# version 02
# baseline (speedup 1.0000x reference)
import contextlib
import numpy as np

import bass_rust
import concourse.bass as bass
import concourse.mybir as mybir
from concourse.bass_utils import run_bass_kernel_spmd

F32 = mybir.dt.float32
BF16 = mybir.dt.bfloat16
U8 = mybir.dt.uint8
AF = mybir.ActivationFunctionType
ALU = mybir.AluOpType
AX = mybir.AxisListType.X

ENGINES = ("sync", "tensor", "vector", "scalar", "gpsimd")


class Cfg:
    def __init__(self, mini=False):
        if mini == "midi":
            self.B_LOC, self.T, self.C, self.HFF, self.E = 2, 1536, 512, 1024, 8
            self.KSEL, self.NBISECT = 4608, 40
        elif mini:
            self.B_LOC, self.T, self.C, self.HFF, self.E = 2, 512, 512, 1024, 8
            self.KSEL, self.NBISECT = 6144, 40
        else:
            self.B_LOC, self.T, self.C, self.HFF, self.E = 2, 2048, 1024, 4096, 8
            self.KSEL, self.NBISECT = 24576, 72
        self.DH = 64
        self.NH = self.C // self.DH
        self.NHP = self.NH // 2
        self.NCT = self.C // 128
        self.NTT = self.T // 128
        self.NQC = self.T // 512
        self.NHT = self.HFF // 128
        self.NDT = self.C // 128
        self.MAXREL = 128
        self.NREL = 2 * self.MAXREL + 1
        self.PW = 511
        self.BT = self.B_LOC * self.T
        self.GTOT = 8 * self.BT * self.E
        self.GL = self.GTOT // 128
        self.NU = self.BT // 512


class Sem:
    __slots__ = ("h", "n", "name")

    def __init__(self, h, name):
        self.h, self.n, self.name = h, 0, name


class Emit:
    def __init__(self, nc, es):
        self.nc, self.es = nc, es
        self.q = {e: [] for e in ENGINES}
        self.sems = []

    def sem(self, name):
        s = Sem(self.es.enter_context(self.nc.semaphore(name)), name)
        self.sems.append(s)
        return s

    def op(self, eng, fn, inc=None, inc_by=1):
        if inc is None:
            self.q[eng].append(lambda E, fn=fn: fn(E))
            return None
        inc.n += inc_by
        self.q[eng].append(lambda E, fn=fn, s=inc, b=inc_by: fn(E).then_inc(s.h, b))
        return inc.n

    def wait(self, eng, sem, val=None):
        v = sem.n if val is None else val
        if v > 0:
            self.q[eng].append(lambda E, s=sem, v=v: E.wait_ge(s.h, v))
        return v

    def dma(self, eng, out, in_, sem, **kw):
        sem.n += 16
        self.q[eng].append(lambda E, o=out, i=in_, s=sem, kw=kw:
                           E.dma_start(out=o, in_=i, **kw).then_inc(s.h, 16))
        return sem.n

    def dma_t(self, eng, out, in_, sem):
        sem.n += 16
        self.q[eng].append(lambda E, o=out, i=in_, s=sem:
                           E.dma_start_transpose(out=o, in_=i).then_inc(s.h, 16))
        return sem.n

    def barrier(self):
        counts = [(s, s.n) for s in self.sems if s.n > 0]
        for eng in ENGINES:
            for s, v in counts:
                self.q[eng].append(lambda E, s=s, v=v: E.wait_ge(s.h, v))


def dap(t_ap, offset, dims):
    return bass_rust.AP(tensor=t_ap.tensor, offset=offset, ap=dims)


def sap(sb_t, offset, free_dims, nparts=128):
    h = sb_t[:].tensor
    row = int(h.shape[1])
    return bass_rust.AP(tensor=h, offset=offset,
                        ap=[[row, nparts]] + free_dims)


def build_program(cfg, n_subl=4, dump_xt=False):
    c = cfg
    NCT, NTT, NQC, NHP, NHT, NDT = c.NCT, c.NTT, c.NQC, c.NHP, c.NHT, c.NDT
    T, C, HFF, E, BT, DH, NH = c.T, c.C, c.HFF, c.E, c.BT, c.DH, c.NH
    PW = c.PW
    VROW = NH * 65
    nc = bass.Bass("TRN2", target_bir_lowering=False, debug=False)

    inp = {}

    def din(name, shape):
        inp[name] = nc.dram_tensor(name, list(shape), F32, kind="ExternalInput").ap()
        return inp[name]

    x_in = din("L0_attn_a_x", (c.B_LOC, T, C))
    for p in "qkvp":
        din(f"L0_attn_a_W{p}", (C, C)); din(f"L0_attn_a_b{p}", (C,))
    for p in "qkvo":
        din(f"L0_attn_b_W{p}", (C, C)); din(f"L0_attn_b_b{p}", (C,))
    din("L0_attn_b_rel_emb", (c.NREL, DH))
    din("L0_ffn_gate_weight", (E, C))
    din("L0_ffn_w1", (E, HFF, C)); din("L0_ffn_b1", (E, HFF))
    din("L0_ffn_w2", (E, C, HFF)); din("L0_ffn_b2", (E, C))
    out = nc.dram_tensor("out", [c.B_LOC, T, C], F32, kind="ExternalOutput").ap()

    xT_d = nc.dram_tensor("xT_d", [c.B_LOC * C * T], F32).ap()
    xnT_d = nc.dram_tensor("xnT_d", [c.B_LOC * C * T], BF16).ap()
    yT_d = nc.dram_tensor("yT_d", [C * T], BF16).ap()
    wnat, wtr = {}, {}
    for fam, ps in (("a", "qkvp"), ("b", "qkvo")):
        for p in ps:
            wnat[fam + p] = nc.dram_tensor(f"w{fam}{p}_nat", [C * C], BF16).ap()
            wtr[fam + p] = nc.dram_tensor(f"w{fam}{p}_t", [C * C], BF16).ap()
    w1_nat = nc.dram_tensor("w1_nat", [E * HFF * C], BF16).ap()
    w2_nat = nc.dram_tensor("w2_nat", [E * C * HFF], BF16).ap()
    w1_t = nc.dram_tensor("w1_t", [E * HFF * C], BF16).ap()
    w2_t = nc.dram_tensor("w2_t", [E * C * HFF], BF16).ap()
    pp_d = [nc.dram_tensor(f"pp_d{i}", [NTT * 128 * PW], BF16).ap() for i in range(2)]
    rc_d = nc.dram_tensor("rc_d", [T], BF16).ap()
    yodd_d = nc.dram_tensor("yodd_d", [64 * T], BF16).ap()
    rel_dup_d = nc.dram_tensor("rel_dup_d", [64 * PW], F32).ap()
    gts_d = nc.dram_tensor("gts_d", [E * BT], F32).ap()
    gtsb_d = nc.dram_tensor("gtsb_d", [E * BT], BF16).ap()
    lg_in = nc.dram_tensor("lg_in", [E * BT], F32).ap()
    lg_out = nc.dram_tensor("lg_out", [8 * E * BT], F32).ap()
    xt_dump = (nc.dram_tensor("xt_dump", [c.B_LOC * C * T], F32,
                              kind="ExternalOutput").ap() if dump_xt else None)

    es = contextlib.ExitStack()
    K = Emit(nc, es)
    with es:
        def sb(stack, name, shape, dt, **kw):
            return stack.enter_context(nc.sbuf_tensor(name, list(shape), dt, **kw))

        cst = sb(es, "cst", (128, 8), F32)
        ones_st = sb(es, "ones_st", (128, 1), BF16)
        ones_row = sb(es, "ones_row", (1, 128), BF16)
        onesf_st = sb(es, "onesf_st", (128, 128), F32)
        id128 = sb(es, "id128", (128, 128), BF16)
        id128f = sb(es, "id128f", (128, 128), F32)
        onetmp = sb(es, "onetmp", (128, 128), F32)
        rel_ext = sb(es, "rel_ext", (128, PW), BF16)
        rel_c = sb(es, "rel_c", (128, 2), F32)
        bias_a = sb(es, "bias_a", (128, 4 * NCT), F32)
        bias_b = sb(es, "bias_b", (128, 4 * NCT), F32)
        bklo = sb(es, "bklo", (128, NCT), F32)
        bkhi = sb(es, "bkhi", (128, NCT), F32)
        b1t = sb(es, "b1t", (128, E * NHT), F32)
        b2bf = sb(es, "b2bf", (8, C), BF16)
        gwT = sb(es, "gwT", (128, 8 * NCT), BF16)
        lgst = sb(es, "lgst", (8, 512), F32)
        bis = sb(es, "bis", (128, 8), F32)
        bcnt = sb(es, "bcnt", (128, 1), F32)
        bprd = sb(es, "bprd", (128, 1), U8)
        bprd2 = sb(es, "bprd2", (128, 1), U8)
        xtb = [sb(es, f"xtb{i}", (128, NCT * 512), F32) for i in range(2)]
        xcb = sb(es, "xcb", (128, NCT * 512), BF16)
        strow = sb(es, "strow", (1, 2 * 512), F32)
        mrbf = sb(es, "mrbf", (1, 2 * 512), BF16)

        s_pe = K.sem("s_pe"); s_dve = K.sem("s_dve"); s_act = K.sem("s_act")
        s_gp = K.sem("s_gp"); s_qk = K.sem("s_qk")
        s_prep = K.sem("s_prep")
        s_tl = [K.sem("s_tl0"), K.sem("s_tl1")]
        s_ts = [K.sem("s_ts0"), K.sem("s_ts1")]
        s_wA = K.sem("s_wA"); s_wB = K.sem("s_wB")
        s_xt = [K.sem("s_xt0"), K.sem("s_xt1")]
        s_xst = K.sem("s_xst")
        s_xn = [K.sem("s_xn0"), K.sem("s_xn1")]
        s_xns = K.sem("s_xns")
        s_pp = K.sem("s_pp"); s_sk = K.sem("s_sk")
        s_bc = K.sem("s_bc")
        s_io = K.sem("s_io")
        s_cc = K.sem("s_cc")

        def mm(o, l, r, start, stop, inc=None):
            K.op("tensor", lambda E, o=o, l=l, r=r, s1=start, s2=stop:
                 E.matmul(o, l, r, start=s1, stop=s2), inc=inc)

        def tpose(o, i, ident, inc=None):
            K.op("tensor", lambda E, o=o, i=i, d=ident: E.transpose(o, i, d), inc=inc)

        def act(o, i, func, bias=0.0, scale=1.0, inc=None):
            K.op("scalar", lambda E, o=o, i=i, f=func, b=bias, s=scale:
                 E.activation(o, i, f, bias=b, scale=s), inc=inc)

        def vcopy(o, i, inc=None):
            K.op("vector", lambda E, o=o, i=i: E.tensor_copy(o, i), inc=inc)

        def vtt(o, a, b, opk, inc=None):
            K.op("vector", lambda E, o=o, x=a, y=b, k=opk:
                 E.tensor_tensor(o, x, y, k), inc=inc)

        def vts(o, i, s1, op0, s2=None, op1=None, inc=None):
            if op1 is None:
                K.op("vector", lambda E, o=o, i=i, a=s1, k=op0:
                     E.tensor_scalar(o, i, a, None, k), inc=inc)
            else:
                K.op("vector", lambda E, o=o, i=i, a=s1, b=s2, k0=op0, k1=op1:
                     E.tensor_scalar(o, i, a, b, k0, k1), inc=inc)

        def vstt(o, i0, scal, i1, op0, op1, inc=None):
            K.op("vector", lambda E, o=o, i=i0, s=scal, j=i1, k0=op0, k1=op1:
                 E.scalar_tensor_tensor(o, i, s, j, k0, k1), inc=inc)

        def vset(a, v, inc=None):
            K.op("vector", lambda E, a=a, v=v: E.memset(a, v), inc=inc)

        def vred(o, i, absval=None, inc=None):
            K.op("vector", lambda E, o=o, i=i, av=absval:
                 E.tensor_reduce(o, i, axis=AX, op=ALU.add, apply_absolute_value=av),
                 inc=inc)

        def vpred(o, m, d, inc=None):
            K.op("vector", lambda E, o=o, m=m, d=d: E.copy_predicated(o, m, d), inc=inc)

        vset(cst[:, 0:1], 0.125)
        vset(cst[:, 1:2], 1.0 / C)
        vset(cst[:, 2:3], 1e-5)
        vset(cst[:, 3:4], -0.5)
        vset(cst[:, 4:5], float(c.KSEL))
        vset(cst[:, 5:6], -1.0)
        vset(cst[:, 6:7], 0.5)
        vset(ones_st[:], 1.0)
        vset(ones_row[:], 1.0)
        vset(onesf_st[:], 1.0)
        vset(onetmp[:], 1.0, inc=s_dve)
        K.wait("gpsimd", s_dve, s_dve.n)
        K.op("gpsimd", lambda E: E.affine_select(
            id128f[:], onetmp[:], pattern=[[1, 128]], compare_op=ALU.is_equal,
            fill=0.0, base=0, channel_multiplier=-1), inc=s_gp)
        K.wait("vector", s_gp, s_gp.n)
        vcopy(id128[:], id128f[:], inc=s_dve)

        esP = contextlib.ExitStack()
        with esP:
            wstage = sb(esP, "wstage", (128, C * NCT), F32)
            wstg_b = sb(esP, "wstg_b", (128, C * NCT), BF16)
            alpha = sb(esP, "alpha", (128, NCT), F32)
            ttmp = [sb(esP, f"ttmp{i}", (128, max(HFF, C)), BF16,
                       is_transpose_dest=True) for i in range(2)]
            relstage = sb(esP, "relstage", (128, 2 * DH), F32)
            relrow = sb(esP, "relrow", (1, DH), F32)
            rel_f = sb(esP, "rel_f", (128, PW), F32)
            b2f = sb(esP, "b2f", (8, C), F32)
            gwTf = sb(esP, "gwTf", (128, 8 * NCT), F32)
            xin = sb(esP, "xin", (128, C), F32)
            xout = sb(esP, "xout", (128, C), F32)
            rel_ps = esP.enter_context(nc.psum_tensor("rel_ps", [128, 384], F32))
            ti_ps = [esP.enter_context(nc.psum_tensor(f"ti_ps{i}", [128, 128], F32))
                     for i in range(2)]

            for p in "qkvo":
                K.dma("gpsimd", dap(wnat["b" + p], 0, [[1, C * C]]),
                      dap(inp[f"L0_attn_b_W{p}"], 0, [[1, C * C]]), s_prep)
            for e in range(E):
                K.dma("gpsimd", dap(w1_nat, e * HFF * C, [[1, HFF * C]]),
                      dap(inp["L0_ffn_w1"], e * HFF * C, [[1, HFF * C]]), s_prep)
                K.dma("gpsimd", dap(w2_nat, e * HFF * C, [[1, HFF * C]]),
                      dap(inp["L0_ffn_w2"], e * HFF * C, [[1, HFF * C]]), s_prep)
            prep_cast_done = s_prep.n

            for fam, btile in (("a", bias_a), ("b", bias_b)):
                for pi, p in enumerate("qkvp" if fam == "a" else "qkvo"):
                    bsrc = inp[f"L0_attn_{fam}_b{p}"]
                    for ot in range(NCT):
                        K.dma("gpsimd", btile[:, pi * NCT + ot: pi * NCT + ot + 1],
                              dap(bsrc, ot * 128, [[1, 128], [1, 1]]), s_bc)
            for e in range(E):
                for ht in range(NHT):
                    K.dma("gpsimd", b1t[:, e * NHT + ht: e * NHT + ht + 1],
                          dap(inp["L0_ffn_b1"], e * HFF + ht * 128,
                              [[1, 128], [1, 1]]), s_bc)
            K.dma("gpsimd", b2f[:], dap(inp["L0_ffn_b2"], 0, [[C, 8], [1, C]]), s_bc)
            rel = inp["L0_attn_b_rel_emb"]
            for half in range(2):
                K.dma("gpsimd", rel_c[64 * half:64 * (half + 1), 0:1],
                      dap(rel, 0, [[1, 64], [1, 1]]), s_bc)
                K.dma("gpsimd", rel_c[64 * half:64 * (half + 1), 1:2],
                      dap(rel, 2 * c.MAXREL * 64, [[1, 64], [1, 1]]), s_bc)
            for ct in range(NCT):
                K.dma("gpsimd", gwTf[:, ct * 8:(ct + 1) * 8],
                      dap(inp["L0_ffn_gate_weight"], ct * 128, [[1, 128], [C, 8]]),
                      s_bc, allow_slow_non_contiguous=True)
            K.dma("gpsimd", relstage[:, 0:DH], dap(rel, 0, [[DH, 128], [1, DH]]),
                  s_bc)
            K.dma("gpsimd", relstage[:, DH:2 * DH],
                  dap(rel, 128 * DH, [[DH, 128], [1, DH]]), s_bc)
            K.dma("gpsimd", relrow[:], dap(rel, 256 * DH, [[DH, 1], [1, DH]]), s_bc)
            smalls_dma = s_bc.n
            K.wait("vector", s_bc, smalls_dma)
            vcopy(b2bf[:], b2f[:])
            vcopy(gwT[:], gwTf[:])
            vts(bklo[:], bias_b[:, NCT:2 * NCT], rel_c[:, 0:1], ALU.add)
            vts(bkhi[:], bias_b[:, NCT:2 * NCT], rel_c[:, 1:2], ALU.add, inc=s_dve)

            K.wait("tensor", s_bc, smalls_dma)
            K.wait("tensor", s_gp, s_gp.n)
            tpose(rel_ps[0:64, 0:128], relstage[:, 0:DH], id128f[:])
            tpose(rel_ps[0:64, 128:256], relstage[:, DH:2 * DH], id128f[:])
            tpose(rel_ps[0:64, 256:257], relrow[:], id128f[0:1, 0:1], inc=s_pe)
            K.wait("vector", s_pe, s_pe.n)
            vcopy(rel_f[0:64, 127:255], rel_ps[0:64, 0:128])
            vcopy(rel_f[0:64, 255:383], rel_ps[0:64, 128:256])
            vcopy(rel_f[0:64, 383:384], rel_ps[0:64, 256:257], inc=s_dve)
            K.wait("vector", s_dve, s_dve.n)
            vts(rel_f[0:64, 0:127], onetmp[0:64, 0:127], rel_f[0:64, 127:128],
                ALU.mult)
            vts(rel_f[0:64, 384:PW], onetmp[0:64, 0:PW - 384],
                rel_f[0:64, 383:384], ALU.mult, inc=s_dve)
            K.wait("scalar", s_dve, s_dve.n)
            K.dma("scalar", dap(rel_dup_d, 0, [[PW, 64], [1, PW]]),
                  rel_f[0:64, :], s_xst)
            K.wait("scalar", s_xst, s_xst.n)
            for half in range(2):
                K.dma("scalar", rel_f[64 * half:64 * (half + 1), :],
                      dap(rel_dup_d, 0, [[PW, 64], [1, PW]]), s_xt[0])
            K.wait("vector", s_xt[0], s_xt[0].n)
            vcopy(rel_ext[:], rel_f[:], inc=s_dve)

            for p in "qkvp":
                w = inp[f"L0_attn_a_W{p}"]
                K.wait("scalar", s_xst, s_xst.n)
                K.wait("scalar", s_dve, s_dve.n)
                K.wait("scalar", s_act, s_act.n)
                K.dma("scalar", wstage[:].rearrange("p (n w) -> p n w", n=NCT),
                      dap(w, 0, [[C, 128], [128 * C, NCT], [1, C]]), s_xt[1])
                K.wait("vector", s_xt[1], s_xt[1].n)
                for ot in range(NCT):
                    vred(alpha[:, ot:ot + 1], wstage[:, ot * C:(ot + 1) * C],
                         absval=True, inc=(s_dve if ot == NCT - 1 else None))
                K.wait("vector", s_dve, s_dve.n)
                vts(alpha[:], alpha[:], 1.0 / C, ALU.mult, inc=s_dve)
                K.wait("scalar", s_dve, s_dve.n)
                for ot in range(NCT):
                    act(wstg_b[:, ot * C:(ot + 1) * C],
                        wstage[:, ot * C:(ot + 1) * C], AF.Sign,
                        inc=(s_act if ot == NCT - 1 else None))
                K.wait("vector", s_act, s_act.n)
                for ot in range(NCT):
                    vts(wstg_b[:, ot * C:(ot + 1) * C],
                        wstg_b[:, ot * C:(ot + 1) * C],
                        alpha[:, ot:ot + 1], ALU.mult,
                        inc=(s_dve if ot == NCT - 1 else None))
                K.wait("scalar", s_dve, s_dve.n)
                K.dma("scalar",
                      dap(wnat["a" + p], 0, [[C, 128], [128 * C, NCT], [1, C]]),
                      wstg_b[:].rearrange("p (n w) -> p n w", n=NCT), s_xst)
            bin_done = s_xst.n

            H1 = HFF * C // 2
            HP1 = HFF * C // 256

            def x_store(kind, dst, dst_off, ct_or_ht, slot, rows):
                v = ttmp[slot][:, 0:rows].rearrange("p (n w) -> p n w", w=128)
                if kind == "block":
                    d = dap(dst, dst_off + ct_or_ht * 128,
                            [[C, 128], [128 * C, rows // 128], [1, 128]])
                elif kind == "ctmaj":
                    d = dap(dst, dst_off + ct_or_ht * C * 128,
                            [[C, 128], [128, rows // 128], [1, 128]])
                elif kind == "w1":
                    ct = ct_or_ht
                    for h2 in range(4):
                        dd = dap(dst, dst_off + h2 * (HFF * C // 4) + ct * 128,
                                 [[HFF * C // 512, 128],
                                  [NCT * 128, NHT // 4], [1, 128]])
                        vv = sap_tt(slot, h2 * (HFF // 4), NHT // 4)
                        K.dma("sync", dd, vv, s_ts[slot])
                    return
                else:
                    ht = ct_or_ht
                    for h2 in range(4):
                        dd = dap(dst, dst_off + h2 * (C * HFF // 4) + ht * 128,
                                 [[HFF * C // 512, 128],
                                  [NHT * 128, NDT // 4], [1, 128]])
                        vv = sap_tt(slot, h2 * (C // 4), NDT // 4)
                        K.dma("sync", dd, vv, s_ts[slot])
                    return
                K.dma("sync", d, v, s_ts[slot])

            def sap_tt(slot, col0, nblk):
                return sap(ttmp[slot], col0, [[128, nblk], [1, 128]])

            def emit_wt(kind, src, src_off, src_rowlen, dst, dst_off, nblk_iter,
                        rows, wait_sem, wait_val):
                for i in range(nblk_iter):
                    slot = i % 2
                    K.wait("sync", wait_sem, wait_val)
                    K.wait("sync", s_ts[slot], s_ts[slot].n)
                    K.dma_t("sync", ttmp[slot][:, 0:rows],
                            dap(src, src_off + i * 128, [[src_rowlen, rows],
                                                         [1, 128]]), s_tl[slot])
                    K.wait("sync", s_tl[slot], s_tl[slot].n)
                    x_store(kind, dst, dst_off, i, slot, rows)

            for p, kind in (("q", "block"), ("k", "block"), ("v", "ctmaj"),
                            ("p", "block")):
                emit_wt(kind, wnat["a" + p], 0, C, wtr["a" + p], 0, NCT, C,
                        s_xst, bin_done)
            for e in range(E):
                emit_wt("w1", w1_nat, e * HFF * C, C, w1_t, e * HFF * C, NCT,
                        HFF, s_prep, prep_cast_done)
                emit_wt("w2", w2_nat, e * C * HFF, HFF, w2_t, e * C * HFF, NHT,
                        C, s_prep, prep_cast_done)
            for p, kind in (("q", "block"), ("k", "block"), ("v", "ctmaj"),
                            ("o", "block")):
                emit_wt(kind, wnat["b" + p], 0, C, wtr["b" + p], 0, NCT, C,
                        s_prep, prep_cast_done)

            for b in range(c.B_LOC):
                for tt in range(NTT):
                    K.wait("sync", s_xst, s_xst.n)
                    K.wait("sync", s_dve, s_dve.n)
                    K.dma("sync", xin[:],
                          dap(x_in, (b * T + tt * 128) * C, [[C, 128], [1, C]]),
                          s_io)
                    K.wait("tensor", s_io, s_io.n)
                    K.wait("tensor", s_dve, s_dve.n)
                    evac_at = {}
                    for ct in range(NCT):
                        pp = ti_ps[ct % 2]
                        if ct >= 2:
                            K.wait("tensor", s_dve, evac_at[ct - 2])
                        tpose(pp[:], xin[:, ct * 128:(ct + 1) * 128], id128f[:],
                              inc=s_pe)
                        K.wait("vector", s_pe, s_pe.n)
                        vcopy(xout[:, ct * 128:(ct + 1) * 128], pp[:], inc=s_dve)
                        evac_at[ct] = s_dve.n
                    K.wait("sync", s_dve, s_dve.n)
                    K.dma("sync", dap(xT_d, b * C * T + tt * 128,
                                      [[T, 128], [128 * T, NCT], [1, 128]]),
                          xout[:].rearrange("p (n w) -> p n w", n=NCT), s_xst)
            K.barrier()

        def emit_ln(b, ln_ps, do_logits, lg_ps):
            m_ps, v_ps, M_ps, R_ps = ln_ps
            for ch in range(T // 512):
                sl = ch % 2
                K.wait("scalar", s_dve, s_dve.n)
                K.wait("scalar", s_pe, s_pe.n)
                K.wait("scalar", s_xst, s_xst.n)
                K.dma("scalar", xtb[sl][:].rearrange("p (n w) -> p n w", n=NCT),
                      dap(xT_d, b * C * T + ch * 512,
                          [[T, 128], [128 * T, NCT], [1, 512]]), s_xt[sl])
                K.wait("vector", s_xt[sl], s_xt[sl].n)
                K.wait("vector", s_pe, s_pe.n)
                K.wait("vector", s_xns, s_xns.n)
                for ct in range(NCT):
                    vcopy(xcb[:, ct * 512:(ct + 1) * 512],
                          xtb[sl][:, ct * 512:(ct + 1) * 512],
                          inc=(s_dve if ct == NCT - 1 else None))
                cast_done = s_dve.n
                K.wait("tensor", s_dve, cast_done)
                for ct in range(NCT):
                    mm(m_ps, ones_st[:], xcb[:, ct * 512:(ct + 1) * 512],
                       start=(ct == 0), stop=(ct == NCT - 1),
                       inc=(s_pe if ct == NCT - 1 else None))
                m_pe = s_pe.n
                K.wait("vector", s_pe, m_pe)
                for ct in range(NCT):
                    vtt(xcb[:, ct * 512:(ct + 1) * 512],
                        xcb[:, ct * 512:(ct + 1) * 512],
                        xcb[:, ct * 512:(ct + 1) * 512], ALU.mult,
                        inc=(s_dve if ct == NCT - 1 else None))
                sq_done = s_dve.n
                K.wait("tensor", s_dve, sq_done)
                for ct in range(NCT):
                    mm(v_ps, ones_st[:], xcb[:, ct * 512:(ct + 1) * 512],
                       start=(ct == 0), stop=(ct == NCT - 1),
                       inc=(s_pe if ct == NCT - 1 else None))
                v_pe = s_pe.n
                K.wait("vector", s_pe, v_pe)
                K.wait("vector", s_act, s_act.n)
                vts(strow[:, 0:512], m_ps, cst[0:1, 1:2], ALU.mult)
                vts(strow[:, 512:1024], v_ps, cst[0:1, 1:2], ALU.mult,
                    inc=s_dve)
                K.wait("vector", s_dve, s_dve.n)
                vcopy(mrbf[:, 0:512], strow[:, 0:512], inc=s_dve)
                K.wait("vector", s_dve, s_dve.n)
                vtt(strow[:, 0:512], strow[:, 0:512], strow[:, 0:512],
                    ALU.mult, inc=s_dve)
                K.wait("vector", s_dve, s_dve.n)
                vtt(strow[:, 512:1024], strow[:, 512:1024],
                    strow[:, 0:512], ALU.subtract, inc=s_dve)
                var_done = s_dve.n
                K.wait("scalar", s_dve, var_done)
                act(strow[:, 0:512], strow[:, 512:1024], AF.Ln,
                    bias=cst[0:1, 2:3], inc=s_act)
                K.wait("scalar", s_act, s_act.n)
                act(strow[:, 512:1024], strow[:, 0:512], AF.Exp,
                    scale=cst[0:1, 3:4], inc=s_act)
                rstd_done = s_act.n
                K.wait("vector", s_act, rstd_done)
                vcopy(mrbf[:, 512:1024], strow[:, 512:1024], inc=s_dve)
                mr_done = s_dve.n
                K.wait("tensor", s_dve, mr_done)
                K.wait("tensor", s_act, s_act.n)
                mm(M_ps[:], ones_row[:], mrbf[:, 0:512], start=True, stop=True)
                mm(R_ps[:], ones_row[:], mrbf[:, 512:1024], start=True, stop=True,
                   inc=s_pe)
                bc_pe = s_pe.n
                K.wait("vector", s_pe, bc_pe)
                for ct in range(NCT):
                    vtt(xtb[sl][:, ct * 512:(ct + 1) * 512],
                        xtb[sl][:, ct * 512:(ct + 1) * 512], M_ps[:],
                        ALU.subtract, inc=(s_dve if ct == NCT - 1 else None))
                K.wait("vector", s_dve, s_dve.n)
                for ct in range(NCT):
                    vtt(xcb[:, ct * 512:(ct + 1) * 512],
                        xtb[sl][:, ct * 512:(ct + 1) * 512], R_ps[:], ALU.mult,
                        inc=(s_dve if ct == NCT - 1 else None))
                xn_done = s_dve.n
                K.wait("scalar", s_dve, xn_done)
                K.dma("scalar", dap(xnT_d, b * C * T + ch * 512,
                                    [[T, 128], [128 * T, NCT], [1, 512]]),
                      xcb[:].rearrange("p (n w) -> p n w", n=NCT), s_xns)
                if do_logits:
                    K.wait("tensor", s_dve, xn_done)
                    for ct in range(NCT):
                        mm(lg_ps[0:8, :], gwT[:, ct * 8:(ct + 1) * 8],
                           xcb[:, ct * 512:(ct + 1) * 512],
                           start=(ct == 0), stop=(ct == NCT - 1),
                           inc=(s_pe if ct == NCT - 1 else None))
                    K.wait("vector", s_pe, s_pe.n)
                    K.wait("vector", s_pp, s_pp.n)
                    vcopy(lgst[:], lg_ps[0:8, :], inc=s_dve)
                    K.wait("scalar", s_dve, s_dve.n)
                    K.dma("scalar",
                          dap(lg_in, b * T + ch * 512, [[BT, 8], [1, 512]]),
                          lgst[:], s_pp)

        def classify(kt, qc):
            d0 = 128 * kt - 512 * qc
            if d0 <= -256:
                return ("LO", None)
            if d0 >= 640:
                return ("HI", None)
            subs = []
            for qs in range(4):
                d = d0 - 128 * qs
                subs.append("LO" if d <= -256 else ("HI" if d >= 256 else "NEAR"))
            return ("MIX", subs)

        phase_ctr = [0]

        def emit_attn(kind):
            esA = contextlib.ExitStack()
            phase_ctr[0] += 1
            px = f"P{phase_ctr[0]}_"
            with esA:
                xnch = [sb(esA, px + f"at_xn{i}", (128, NCT * 512), BF16)
                        for i in range(2)]
                vps = sb(esA, px + "at_v", (128, NTT * VROW), BF16)
                qkw = [sb(esA, px + f"at_qkw{i}", (128, 2 * NCT * 128), BF16)
                       for i in range(2)]
                qts = sb(esA, px + "at_qt", (128, T), BF16)
                kts = sb(esA, px + "at_kt", (128, T), BF16)
                klo = sb(esA, px + "at_klo", (128, T), BF16) if kind == "b" else None
                khi = sb(esA, px + "at_khi", (128, T), BF16) if kind == "b" else None
                expt = sb(esA, px + "at_exp", (128, 8 * 512), BF16)
                skw = sb(esA, px + "at_skw", (128, 4 * 384), BF16) if kind == "b" else None
                ppsb = [sb(esA, px + f"at_pp{i}", (128, PW), BF16) for i in range(2)] \
                    if kind == "b" else None
                yraw = sb(esA, px + "at_yraw", (128, T), BF16)
                rcp = sb(esA, px + "at_rcp", (128, T), BF16)
                rbr = sb(esA, px + "at_rbr", (128, T), BF16)
                ytmp = sb(esA, px + "at_ytmp", (128, T), BF16)
                ytslab = sb(esA, px + "at_ytsl", (128, T), BF16)
                ymv = [sb(esA, px + f"at_ymv{i}", (128, NCT * 512), BF16)
                       for i in range(2)]
                bvb = sb(esA, px + "at_bv", (128, C), F32)
                wvv = sb(esA, px + "at_wv", (128, NCT * C), BF16)
                wpj = [sb(esA, px + f"at_wp{i}", (128, NCT * 128), BF16)
                       for i in range(2)]
                M_ps = esA.enter_context(nc.psum_tensor(px + "atp_M", [128, 512], F32))
                R_ps = esA.enter_context(nc.psum_tensor(px + "atp_R", [128, 512], F32))
                lg_ps = esA.enter_context(nc.psum_tensor(px + "atp_lg", [128, 512], F32))
                pj_ps = [esA.enter_context(
                    nc.psum_tensor(px + f"atp_pj{i}", [128, 512], F32)) for i in range(2)]
                ln_ps = (M_ps[0:1, :], R_ps[0:1, :], M_ps, R_ps)

                wq = wtr[kind + "q"]; wk = wtr[kind + "k"]
                wv = wtr[kind + "v"]
                wp = wtr[kind + ("p" if kind == "a" else "o")]
                btile = bias_a if kind == "a" else bias_b

                bvin = inp[f"L0_attn_{kind}_bv"]
                K.dma("gpsimd", bvb[:], dap(bvin, 0, [[0, 128], [1, C]]), s_bc)
                bv_ld = s_bc.n
                K.dma("sync", wvv[:].rearrange("p (n w) -> p n w", n=NCT),
                      dap(wv, 0, [[C, 128], [128 * C, NCT], [1, C]]), s_wA)
                wv_ld = s_wA.n

                for b in range(c.B_LOC):
                    emit_ln(b, ln_ps, False, lg_ps)
                    K.wait("vector", s_pe, s_pe.n)
                    vset(sap(vps, 64, [[VROW, NTT], [65, NH]]), 1.0,
                         inc=s_dve)
                    vones = s_dve.n
                    K.wait("tensor", s_wA, wv_ld)
                    for ch in range(T // 512):
                        sl = ch % 2
                        K.wait("scalar", s_pe, s_pe.n)
                        K.wait("scalar", s_xns, s_xns.n)
                        K.dma("scalar",
                              xnch[sl][:].rearrange("p (n w) -> p n w", n=NCT),
                              dap(xnT_d, b * C * T + ch * 512,
                                  [[T, 128], [128 * T, NCT], [1, 512]]), s_xn[sl])
                        K.wait("tensor", s_xn[sl], s_xn[sl].n)
                        for ttl in range(4):
                            tt = ch * 4 + ttl
                            for s2 in range(C // 512):
                                pp = pj_ps[(ttl * (C // 512) + s2) % 2]
                                K.wait("tensor", s_dve, s_dve.n)
                                for ct in range(NCT):
                                    mm(pp[:],
                                       xnch[sl][:, ct * 512 + ttl * 128:
                                                ct * 512 + ttl * 128 + 128],
                                       wvv[:, ct * C + s2 * 512:
                                           ct * C + s2 * 512 + 512],
                                       start=(ct == 0), stop=(ct == NCT - 1),
                                       inc=(s_pe if ct == NCT - 1 else None))
                                K.wait("vector", s_pe, s_pe.n)
                                K.wait("vector", s_dve, vones)
                                K.wait("vector", s_bc, bv_ld)
                                vtt(sap(vps, tt * VROW + s2 * 8 * 65,
                                        [[65, 512 // 64], [1, 64]]),
                                    pp[:].rearrange("p (n w) -> p n w", n=8),
                                    bvb[:, s2 * 512:(s2 + 1) * 512]
                                    .rearrange("p (n w) -> p n w", n=8),
                                    ALU.add, inc=s_dve)
                    vdone = s_dve.n
                    sc_guard = {0: 0, 1: 0}
                    for hp in range(NHP):
                        slw = hp % 2
                        K.wait("sync", s_pe, s_pe.n)
                        K.dma("sync", qkw[slw][:, 0:NCT * 128],
                              dap(wq, hp * 128 * C, [[C, 128], [1, NCT * 128]]),
                              s_wB)
                        K.dma("sync", qkw[slw][:, NCT * 128:2 * NCT * 128],
                              dap(wk, hp * 128 * C, [[C, 128], [1, NCT * 128]]),
                              s_wB)
                        qk_ld = s_wB.n
                        K.wait("tensor", s_wB, qk_ld)
                        K.wait("tensor", s_dve, vdone)
                        for qc in range(NQC):
                            sl = qc % 2
                            K.wait("scalar", s_pe, s_pe.n)
                            K.wait("scalar", s_xns, s_xns.n)
                            K.dma("scalar",
                                  xnch[sl][:].rearrange("p (n w) -> p n w", n=NCT),
                                  dap(xnT_d, b * C * T + qc * 512,
                                      [[T, 128], [128 * T, NCT], [1, 512]]),
                                  s_xn[sl])
                            K.wait("tensor", s_xn[sl], s_xn[sl].n)
                            for which in (0, 1):
                                pp = pj_ps[which]
                                K.wait("tensor", s_act, s_act.n)
                                for ct in range(NCT):
                                    mm(pp[:],
                                       qkw[slw][:, which * NCT * 128 + ct * 128:
                                                which * NCT * 128 + ct * 128 + 128],
                                       xnch[sl][:, ct * 512:(ct + 1) * 512],
                                       start=(ct == 0), stop=(ct == NCT - 1),
                                       inc=(s_pe if ct == NCT - 1 else None))
                                K.wait("scalar", s_pe, s_pe.n)
                                dst = qts if which == 0 else kts
                                bcol = btile[:, which * NCT + hp:
                                             which * NCT + hp + 1]
                                act(dst[:, qc * 512:(qc + 1) * 512], pp[:],
                                    AF.Identity, bias=bcol, inc=s_act)
                                if kind == "b" and which == 1:
                                    act(klo[:, qc * 512:(qc + 1) * 512], pp[:],
                                        AF.Identity, bias=bklo[:, hp:hp + 1],
                                        inc=s_act)
                                    act(khi[:, qc * 512:(qc + 1) * 512], pp[:],
                                        AF.Identity, bias=bkhi[:, hp:hp + 1],
                                        inc=s_act)
                        qk_act = s_act.n
                        for hi in range(2):
                            h = 2 * hp + hi
                            r0 = 64 * hi
                            K.wait("tensor", s_act, qk_act)
                            if kind == "b":
                                ppd = pp_d[hi]
                                K.wait("tensor", s_act, qk_act)
                                K.wait("scalar", s_sk, s_sk.n)
                                for qt in range(NTT):
                                    pp = pj_ps[qt % 2]
                                    psl = ppsb[qt % 2]
                                    K.wait("tensor", s_dve, s_dve.n)
                                    K.wait("tensor", s_pp, s_pp.n)
                                    mm(pp[:, 0:PW],
                                       qts[r0:r0 + 64, qt * 128:(qt + 1) * 128],
                                       rel_ext[r0:r0 + 64, :], start=True,
                                       stop=True, inc=s_pe)
                                    K.wait("vector", s_pe, s_pe.n)
                                    K.wait("vector", s_pp, s_pp.n)
                                    vcopy(psl[:, 0:PW], pp[:, 0:PW], inc=s_dve)
                                    K.wait("scalar", s_dve, s_dve.n)
                                    K.dma("scalar",
                                          dap(ppd, qt * 128 * PW,
                                              [[PW, 128], [1, PW]]),
                                          psl[:, 0:PW], s_pp)
                                pp_done = s_pp.n
                            for qc in range(NQC):
                                if kind == "b":
                                    K.wait("scalar", s_pp, pp_done)
                                    K.wait("scalar", s_qk, s_qk.n)
                                    for qs in range(4):
                                        qt = qc * 4 + qs
                                        K.dma("scalar",
                                              skw[:, qs * 384:(qs + 1) * 384],
                                              dap(ppd, qt * 128 * PW + 127,
                                                  [[PW - 1, 128], [1, 384]]),
                                              s_sk)
                                    sk_ld = s_sk.n
                                av_ps = pj_ps[qc % 2]
                                sc_ps = [M_ps, R_ps]
                                exp_acts = {}
                                av_guard = s_pe.n
                                for kt in range(NTT):
                                    sp = sc_ps[kt % 2]
                                    K.wait("tensor", s_act, sc_guard[kt % 2])
                                    if kind == "a":
                                        case, subs = "PLAIN", None
                                    else:
                                        case, subs = classify(kt, qc)
                                    if case in ("PLAIN", "LO", "HI"):
                                        src = (kts if case == "PLAIN" else
                                               (klo if case == "LO" else khi))
                                        mm(sp[:],
                                           src[r0:r0 + 64, kt * 128:(kt + 1) * 128],
                                           qts[r0:r0 + 64, qc * 512:(qc + 1) * 512],
                                           start=True, stop=True, inc=s_qk)
                                    else:
                                        K.wait("tensor", s_sk, sk_ld)
                                        items = []
                                        for qs in range(4):
                                            q0 = qc * 512 + qs * 128
                                            sub = subs[qs]
                                            src = (klo if sub == "LO" else
                                                   khi if sub == "HI" else kts)
                                            items.append((
                                                sp[:, qs * 128:(qs + 1) * 128],
                                                src[r0:r0 + 64,
                                                    kt * 128:(kt + 1) * 128],
                                                qts[r0:r0 + 64, q0:q0 + 128]))
                                            if sub == "NEAR":
                                                off = 128 * kt - (q0 - 128)
                                                items.append((
                                                    sp[:, qs * 128:(qs + 1) * 128],
                                                    skw[:, qs * 384 + off:
                                                        qs * 384 + off + 128],
                                                    id128[:]))
                                        for ii, (o_, l_, r_) in enumerate(items):
                                            mm(o_, l_, r_, start=(ii == 0),
                                               stop=(ii == len(items) - 1),
                                               inc=(s_qk if ii == len(items) - 1
                                                    else None))
                                    K.wait("scalar", s_qk, s_qk.n)
                                    K.wait("scalar", s_pe, av_guard)
                                    act(expt[:, (kt % 8) * 512:(kt % 8) * 512 + 512],
                                        sp[:], AF.Exp, scale=cst[:, 0:1],
                                        inc=s_act)
                                    exp_acts[kt] = s_act.n
                                    sc_guard[kt % 2] = s_act.n
                                    if kt % 8 == 7 or kt == NTT - 1:
                                        K.wait("tensor", s_act, exp_acts[kt])
                                        K.wait("tensor", s_dve, s_dve.n)
                                        for k2 in range(kt - (kt % 8), kt + 1):
                                            mm(av_ps[0:65, :],
                                               sap(vps, k2 * VROW + h * 65,
                                                   [[1, 65]]),
                                               expt[:, (k2 % 8) * 512:
                                                    (k2 % 8) * 512 + 512],
                                               start=(k2 == 0),
                                               stop=(k2 == NTT - 1),
                                               inc=(s_pe if k2 == NTT - 1
                                                    else None))
                                        av_guard = s_pe.n
                                av_pe = s_pe.n
                                K.wait("vector", s_pe, av_pe)
                                K.wait("vector", s_sk, s_sk.n)
                                vcopy(yraw[0:64, qc * 512:(qc + 1) * 512],
                                      av_ps[0:64, :], inc=s_dve)
                                K.wait("scalar", s_pe, av_pe)
                                K.wait("scalar", s_act, s_act.n)
                                act(rcp[64:65, qc * 512:(qc + 1) * 512],
                                    av_ps[64:65, :], AF.Ln, inc=s_act)
                                K.wait("scalar", s_act, s_act.n)
                                act(rcp[64:65, qc * 512:(qc + 1) * 512],
                                    rcp[64:65, qc * 512:(qc + 1) * 512], AF.Exp,
                                    scale=cst[64:65, 5:6], inc=s_act)
                            K.wait("scalar", s_act, s_act.n)
                            K.wait("scalar", s_sk, s_sk.n)
                            K.dma("scalar", dap(rc_d, 0, [[T, 1], [1, T]]),
                                  rcp[64:65, :], s_pp)
                            K.wait("scalar", s_pp, s_pp.n)
                            K.dma("scalar", rbr[0:64, :],
                                  dap(rc_d, 0, [[0, 64], [1, T]]), s_sk)
                            rb_ld = s_sk.n
                            K.wait("vector", s_sk, rb_ld)
                            K.wait("vector", s_dve, s_dve.n)
                            if hi == 0:
                                vtt(ytslab[0:64, :], yraw[0:64, :], rbr[0:64, :],
                                    ALU.mult, inc=s_dve)
                            else:
                                vtt(ytmp[0:64, :], yraw[0:64, :], rbr[0:64, :],
                                    ALU.mult, inc=s_dve)
                                K.wait("scalar", s_dve, s_dve.n)
                                K.dma("scalar",
                                      dap(yodd_d, 0, [[T, 64], [1, T]]),
                                      ytmp[0:64, :], s_pp)
                                K.wait("scalar", s_pp, s_pp.n)
                                K.dma("scalar", ytslab[64:128, :],
                                      dap(yodd_d, 0, [[T, 64], [1, T]]), s_sk)
                                ysl_ld = s_sk.n
                                K.wait("scalar", s_sk, ysl_ld)
                                K.wait("scalar", s_dve, s_dve.n)
                                K.dma("scalar",
                                      dap(yT_d, hp * 128 * T, [[T, 128], [1, T]]),
                                      ytslab[:], s_xns)
                    yt_done = s_xns.n
                    for cc in range(T // 512):
                        slx = cc % 2
                        K.wait("scalar", s_dve, s_dve.n)
                        K.wait("scalar", s_pe, s_pe.n)
                        K.wait("scalar", s_xst, s_xst.n)
                        K.dma("scalar",
                              xtb[slx][:].rearrange("p (n w) -> p n w", n=NCT),
                              dap(xT_d, b * C * T + cc * 512,
                                  [[T, 128], [128 * T, NCT], [1, 512]]),
                              s_xt[slx])
                        K.wait("scalar", s_xns, yt_done)
                        K.dma("scalar",
                              ymv[slx][:].rearrange("p (n w) -> p n w", n=NCT),
                              dap(yT_d, cc * 512,
                                  [[T, 128], [128 * T, NCT], [1, 512]]),
                              s_xn[slx])
                        for cpt in range(NCT):
                            slw2 = cpt % 2
                            K.wait("sync", s_pe, s_pe.n)
                            K.dma("sync", wpj[slw2][:],
                                  dap(wp, cpt * 128 * C,
                                      [[C, 128], [1, NCT * 128]]), s_wB)
                            K.wait("tensor", s_wB, s_wB.n)
                            K.wait("tensor", s_xn[slx], s_xn[slx].n)
                            K.wait("tensor", s_dve, s_dve.n)
                            K.wait("tensor", s_act, s_act.n)
                            pp = pj_ps[cpt % 2]
                            for ct in range(NCT):
                                mm(pp[:], wpj[slw2][:, ct * 128:(ct + 1) * 128],
                                   ymv[slx][:, ct * 512:(ct + 1) * 512],
                                   start=(ct == 0), stop=(ct == NCT - 1),
                                   inc=(s_pe if ct == NCT - 1 else None))
                            K.wait("vector", s_pe, s_pe.n)
                            K.wait("vector", s_xt[slx], s_xt[slx].n)
                            bcol = btile[:, 3 * NCT + cpt: 3 * NCT + cpt + 1]
                            vstt(xtb[slx][:, cpt * 512:(cpt + 1) * 512], pp[:],
                                 bcol, xtb[slx][:, cpt * 512:(cpt + 1) * 512],
                                 ALU.add, ALU.add, inc=s_dve)
                        K.wait("scalar", s_dve, s_dve.n)
                        K.dma("scalar", dap(xT_d, b * C * T + cc * 512,
                                            [[T, 128], [128 * T, NCT], [1, 512]]),
                              xtb[slx][:].rearrange("p (n w) -> p n w", n=NCT),
                              s_xst)
                K.barrier()

        def emit_ffn():
            esF = contextlib.ExitStack()
            phase_ctr[0] += 1
            px = f"P{phase_ctr[0]}_"
            with esF:
                xnu = [sb(esF, px + f"ff_xn{i}", (128, NCT * 512), BF16)
                       for i in range(2)]
                ght = sb(esF, px + "ff_gh", (128, NHT * 512), BF16)
                accs = sb(esF, px + "ff_acc", (128, NCT * 512), F32)
                gG = sb(esF, px + "ff_g", (128, 512), F32)
                gbu = [sb(esF, px + f"ff_gb{i}", (8, 512), BF16) for i in range(2)]
                glog = sb(esF, px + "ff_glog", (128, c.GL), F32)
                bisv = sb(esF, px + "ff_bisv", (128, c.GL), F32)
                gcb = sb(esF, px + "ff_gcb", (8, 2048), BF16)
                wslot = [sb(esF, px + f"ff_w{i}", (128, HFF * C // 512), BF16)
                         for i in range(2)]
                M_ps = esF.enter_context(nc.psum_tensor(px + "ffp_M", [128, 512], F32))
                R_ps = esF.enter_context(nc.psum_tensor(px + "ffp_R", [128, 512], F32))
                lg_ps = esF.enter_context(nc.psum_tensor(px + "ffp_lg", [128, 512], F32))
                hps = [esF.enter_context(
                    nc.psum_tensor(px + f"ffp_h{i}", [128, 512], F32)) for i in range(2)]
                ops = [esF.enter_context(
                    nc.psum_tensor(px + f"ffp_o{i}", [128, 512], F32)) for i in range(2)]
                cn_ps = lg_ps[0:128, 0:1]
                ln_ps = (M_ps[0:1, :], R_ps[0:1, :], M_ps, R_ps)

                for b in range(c.B_LOC):
                    emit_ln(b, ln_ps, True, lg_ps)
                K.wait("gpsimd", s_pp, s_pp.n)
                K.op("gpsimd", lambda E: E.collective_compute(
                    "AllGather", ALU.bypass, replica_groups=[list(range(8))],
                    ins=[dap(lg_in, 0, [[BT, 8], [1, BT]])],
                    outs=[dap(lg_out, 0, [[BT, 64], [1, BT]])]), inc=s_cc)
                K.wait("gpsimd", s_cc, 1 if s_cc.n <= 1 else s_cc.n)
                K.dma("gpsimd", glog[:],
                      dap(lg_out, 0, [[c.GL, 128], [1, c.GL]]), s_bc)
                gl_ld = s_bc.n
                K.wait("vector", s_bc, gl_ld)
                vset(bis[:, 0:1], -64.0)
                vset(bis[:, 1:2], 64.0, inc=s_dve)
                for it in range(c.NBISECT):
                    K.wait("vector", s_dve, s_dve.n)
                    vtt(bis[:, 2:3], bis[:, 0:1], bis[:, 1:2], ALU.add, inc=s_dve)
                    K.wait("vector", s_dve, s_dve.n)
                    vts(bis[:, 2:3], bis[:, 2:3], cst[:, 6:7], ALU.mult,
                        inc=s_dve)
                    K.wait("vector", s_dve, s_dve.n)
                    vts(bisv[:], glog[:], bis[:, 2:3], ALU.is_le, inc=s_dve)
                    K.wait("vector", s_dve, s_dve.n)
                    vred(bcnt[:], bisv[:], inc=s_dve)
                    cnt_dve = s_dve.n
                    K.wait("tensor", s_dve, cnt_dve)
                    mm(cn_ps, onesf_st[:], bcnt[:], start=True, stop=True,
                       inc=s_pe)
                    K.wait("vector", s_pe, s_pe.n)
                    vcopy(bis[:, 3:4], cn_ps, inc=s_dve)
                    K.wait("vector", s_dve, s_dve.n)
                    vts(bprd[:], bis[:, 3:4], cst[:, 4:5], ALU.is_ge)
                    vts(bprd2[:], bis[:, 3:4], cst[:, 4:5], ALU.is_lt, inc=s_dve)
                    K.wait("vector", s_dve, s_dve.n)
                    vpred(bis[:, 1:2], bprd[:], bis[:, 2:3])
                    vpred(bis[:, 0:1], bprd2[:], bis[:, 2:3], inc=s_dve)
                SEG = min(2048, BT, c.GL)
                for si in range(BT // SEG):
                    K.wait("gpsimd", s_pp, s_pp.n)
                    K.wait("gpsimd", s_dve, s_dve.n)
                    K.dma("gpsimd", glog[0:8, 0:SEG],
                          dap(lg_in, si * SEG, [[BT, 8], [1, SEG]]), s_bc)
                    K.wait("vector", s_bc, s_bc.n)
                    vts(bisv[0:8, 0:SEG], glog[0:8, 0:SEG], bis[0:8, 0:1],
                        ALU.is_gt, inc=s_dve)
                    K.wait("vector", s_dve, s_dve.n)
                    vtt(glog[0:8, 0:SEG], glog[0:8, 0:SEG],
                        bisv[0:8, 0:SEG], ALU.mult, inc=s_dve)
                    K.wait("vector", s_dve, s_dve.n)
                    vcopy(gcb[:, 0:SEG], glog[0:8, 0:SEG], inc=s_dve)
                    K.wait("scalar", s_dve, s_dve.n)
                    K.dma("scalar", dap(gts_d, si * SEG, [[BT, 8], [1, SEG]]),
                          glog[0:8, 0:SEG], s_pp)
                    K.dma("scalar", dap(gtsb_d, si * SEG, [[BT, 8], [1, SEG]]),
                          gcb[:, 0:SEG], s_pp)
                gates_dve = s_dve.n
                gts_done = s_pp.n

                QSZ = HFF * C // 4
                QP = HFF * C // 512
                slot_seq = []
                for u in range(c.NU):
                    for e in range(E):
                        for q in range(4):
                            slot_seq.append((w1_t, e * HFF * C + q * QSZ))
                        for q in range(4):
                            slot_seq.append((w2_t, e * C * HFF + q * QSZ))
                slot_free_pe = [0, 0]
                slot_sem = [s_wA, s_wB]
                slot_loaded = [0, 0]
                seq_i = [0]

                def issue_load():
                    i = seq_i[0]
                    if i >= len(slot_seq):
                        return
                    seq_i[0] = i + 1
                    slx = i % 2
                    srcw, off = slot_seq[i]
                    K.wait("sync", s_pe, slot_free_pe[slx])
                    K.dma("sync", wslot[slx][:],
                          dap(srcw, off, [[QP, 128], [1, QP]]), slot_sem[slx])
                    slot_loaded[slx] = slot_sem[slx].n

                issue_load(); issue_load()

                for u in range(c.NU):
                    b, ch = u // (T // 512), u % (T // 512)
                    su = u % 2
                    K.wait("scalar", s_pe, s_pe.n)
                    K.wait("scalar", s_xns, s_xns.n)
                    K.dma("scalar", xnu[su][:].rearrange("p (n w) -> p n w", n=NCT),
                          dap(xnT_d, b * C * T + ch * 512,
                              [[T, 128], [128 * T, NCT], [1, 512]]), s_xn[su])
                    K.wait("scalar", s_pp, gts_done)
                    K.dma("scalar", gbu[su][:],
                          dap(gtsb_d, b * T + ch * 512, [[BT, 8], [1, 512]]),
                          s_xn[su])
                    xn_ld = s_xn[su].n
                    K.wait("scalar", s_dve, s_dve.n)
                    K.wait("scalar", s_xst, s_xst.n)
                    K.dma("scalar", accs[:].rearrange("p (n w) -> p n w", n=NCT),
                          dap(xT_d, b * C * T + ch * 512,
                              [[T, 128], [128 * T, NCT], [1, 512]]), s_xt[0])
                    acc_ld = s_xt[0].n
                    for e in range(E):
                        K.wait("gpsimd", s_dve, s_dve.n)
                        K.wait("gpsimd", s_pp, gts_done)
                        K.dma("gpsimd", gG[:],
                              dap(gts_d, e * BT + b * T + ch * 512,
                                  [[0, 128], [1, 512]]), s_bc)
                        g_ld = s_bc.n
                        for qtr in range(4):
                            slx = (u * E * 8 + e * 8 + qtr) % 2
                            K.wait("tensor", slot_sem[slx], slot_loaded[slx])
                            if qtr == 0 and e == 0:
                                K.wait("tensor", s_xn[su], xn_ld)
                            for htl in range(NHT // 4):
                                ht = qtr * (NHT // 4) + htl
                                hpp = hps[ht % 2]
                                K.wait("tensor", s_act, s_act.n)
                                for ct in range(NCT):
                                    mm(hpp[:],
                                       wslot[slx][:, (htl * NCT + ct) * 128:
                                                  (htl * NCT + ct) * 128 + 128],
                                       xnu[su][:, ct * 512:(ct + 1) * 512],
                                       start=(ct == 0), stop=(ct == NCT - 1),
                                       inc=(s_pe if ct == NCT - 1 else None))
                                K.wait("scalar", s_pe, s_pe.n)
                                K.wait("scalar", s_dve, s_dve.n)
                                act(ght[:, ht * 512:(ht + 1) * 512], hpp[:],
                                    AF.Gelu_apprx_tanh,
                                    bias=b1t[:, e * NHT + ht: e * NHT + ht + 1],
                                    inc=s_act)
                                K.wait("vector", s_act, s_act.n)
                                K.wait("vector", s_bc, g_ld)
                                vtt(ght[:, ht * 512:(ht + 1) * 512],
                                    ght[:, ht * 512:(ht + 1) * 512], gG[:],
                                    ALU.mult, inc=s_dve)
                            slot_free_pe[slx] = s_pe.n
                            issue_load()
                        gh_done = s_dve.n
                        K.wait("tensor", s_dve, gh_done)
                        for qtr in range(4):
                            slx = (u * E * 8 + e * 8 + 4 + qtr) % 2
                            K.wait("tensor", slot_sem[slx], slot_loaded[slx])
                            for dtl in range(NDT // 4):
                                dt = qtr * (NDT // 4) + dtl
                                op = ops[dt % 2]
                                K.wait("tensor", s_dve, s_dve.n)
                                if e == 0:
                                    K.wait("tensor", s_xn[su], xn_ld)
                                    mm(op[:], b2bf[0:8, dt * 128:(dt + 1) * 128],
                                       gbu[su][:],
                                       start=True, stop=False)
                                for ht in range(NHT):
                                    mm(op[:],
                                       wslot[slx][:, (dtl * NHT + ht) * 128:
                                                  (dtl * NHT + ht) * 128 + 128],
                                       ght[:, ht * 512:(ht + 1) * 512],
                                       start=(ht == 0 and e != 0),
                                       stop=(ht == NHT - 1),
                                       inc=(s_pe if ht == NHT - 1 else None))
                                K.wait("vector", s_pe, s_pe.n)
                                if e == 0:
                                    K.wait("vector", s_xt[0], acc_ld)
                                vtt(accs[:, dt * 512:(dt + 1) * 512], op[:],
                                    accs[:, dt * 512:(dt + 1) * 512], ALU.add,
                                    inc=s_dve)
                            slot_free_pe[slx] = s_pe.n
                            issue_load()
                    K.wait("scalar", s_dve, s_dve.n)
                    K.dma("scalar", dap(xT_d, b * C * T + ch * 512,
                                        [[T, 128], [128 * T, NCT], [1, 512]]),
                          accs[:].rearrange("p (n w) -> p n w", n=NCT), s_xst)
            K.barrier()

        plan = ["a", "f", "b", "f"][:n_subl]
        for ph in plan:
            if ph == "f":
                emit_ffn()
            else:
                emit_attn(ph)

        esO = contextlib.ExitStack()
        with esO:
            to_ps = [esO.enter_context(nc.psum_tensor(f"to_ps{i}", [128, 128], F32))
                     for i in range(2)]
            for b in range(c.B_LOC):
                for tt in range(NTT):
                    K.wait("sync", s_io, s_io.n)
                    K.wait("sync", s_dve, s_dve.n)
                    K.wait("sync", s_xst, s_xst.n)
                    K.dma("sync", xin[:].rearrange("p (n w) -> p n w", n=NCT),
                          dap(xT_d, b * C * T + tt * 128,
                              [[T, 128], [128 * T, NCT], [1, 128]]), s_xt[1])
                    K.wait("tensor", s_xt[1], s_xt[1].n)
                    K.wait("tensor", s_dve, s_dve.n)
                    evac_at = {}
                    for ct in range(NCT):
                        pp = to_ps[ct % 2]
                        if ct >= 2:
                            K.wait("tensor", s_dve, evac_at[ct - 2])
                        tpose(pp[:], xin[:, ct * 128:(ct + 1) * 128], id128f[:],
                              inc=s_pe)
                        K.wait("vector", s_pe, s_pe.n)
                        vcopy(xout[:, ct * 128:(ct + 1) * 128], pp[:], inc=s_dve)
                        evac_at[ct] = s_dve.n
                    K.wait("sync", s_dve, s_dve.n)
                    K.dma("sync",
                          dap(out, (b * T + tt * 128) * C, [[C, 128], [1, C]]),
                          xout[:], s_io)
            if dump_xt:
                K.wait("sync", s_xst, s_xst.n)
                K.dma("sync", dap(xt_dump, 0, [[1, c.B_LOC * C * T]]),
                      dap(xT_d, 0, [[1, c.B_LOC * C * T]]), s_io)
            K.wait("sync", s_io, s_io.n)

        with nc.Block() as block:
            @block.sync
            def _(E):
                for f in K.q["sync"]:
                    f(E)

            @block.tensor
            def _(E):
                for f in K.q["tensor"]:
                    f(E)

            @block.vector
            def _(E):
                for f in K.q["vector"]:
                    f(E)

            @block.scalar
            def _(E):
                for f in K.q["scalar"]:
                    f(E)

            @block.gpsimd
            def _(E):
                for f in K.q["gpsimd"]:
                    f(E)
    return nc


def kernel(**inputs):
    cfg = Cfg(mini=False)
    nc = build_program(cfg, n_subl=4)
    x = np.ascontiguousarray(inputs["L0_attn_a_x"])
    shards = np.split(x, 8, axis=0)
    base = {k: np.ascontiguousarray(v) for k, v in inputs.items()
            if k != "L0_attn_a_x"}
    in_maps = [dict(base, L0_attn_a_x=shards[i]) for i in range(8)]
    res = run_bass_kernel_spmd(nc, in_maps, core_ids=list(range(8)))
    return np.concatenate([res.results[i]["out"] for i in range(8)], axis=0)


# version 03
# speedup vs baseline: 1.1855x; 1.1855x over previous
import contextlib
import numpy as np

import bass_rust
import concourse.bass as bass
import concourse.mybir as mybir
from concourse.bass_utils import run_bass_kernel_spmd

F32 = mybir.dt.float32
BF16 = mybir.dt.bfloat16
U8 = mybir.dt.uint8
AF = mybir.ActivationFunctionType
ALU = mybir.AluOpType
AX = mybir.AxisListType.X

ENGINES = ("sync", "tensor", "vector", "scalar", "gpsimd")


class Cfg:
    def __init__(self, mini=False):
        if mini == "midi":
            self.B_LOC, self.T, self.C, self.HFF, self.E = 2, 1536, 512, 1024, 8
            self.KSEL, self.NBISECT = 4608, 40
        elif mini:
            self.B_LOC, self.T, self.C, self.HFF, self.E = 2, 512, 512, 1024, 8
            self.KSEL, self.NBISECT = 6144, 40
        else:
            self.B_LOC, self.T, self.C, self.HFF, self.E = 2, 2048, 1024, 4096, 8
            self.KSEL, self.NBISECT = 24576, 72
        self.DH = 64
        self.NH = self.C // self.DH
        self.NHP = self.NH // 2
        self.NCT = self.C // 128
        self.NTT = self.T // 128
        self.NQC = self.T // 512
        self.NHT = self.HFF // 128
        self.NDT = self.C // 128
        self.MAXREL = 128
        self.NREL = 2 * self.MAXREL + 1
        self.PW = 511
        self.BT = self.B_LOC * self.T
        self.GTOT = 8 * self.BT * self.E
        self.GL = self.GTOT // 128
        self.NU = self.BT // 512


class Sem:
    __slots__ = ("h", "n", "name")

    def __init__(self, h, name):
        self.h, self.n, self.name = h, 0, name


class Emit:
    def __init__(self, nc, es):
        self.nc, self.es = nc, es
        self.q = {e: [] for e in ENGINES}
        self.sems = []

    def sem(self, name):
        s = Sem(self.es.enter_context(self.nc.semaphore(name)), name)
        self.sems.append(s)
        return s

    def op(self, eng, fn, inc=None, inc_by=1):
        if inc is None:
            self.q[eng].append(lambda E, fn=fn: fn(E))
            return None
        inc.n += inc_by
        self.q[eng].append(lambda E, fn=fn, s=inc, b=inc_by: fn(E).then_inc(s.h, b))
        return inc.n

    def wait(self, eng, sem, val=None):
        v = sem.n if val is None else val
        if v > 0:
            self.q[eng].append(lambda E, s=sem, v=v: E.wait_ge(s.h, v))
        return v

    def dma(self, eng, out, in_, sem, **kw):
        sem.n += 16
        self.q[eng].append(lambda E, o=out, i=in_, s=sem, kw=kw:
                           E.dma_start(out=o, in_=i, **kw).then_inc(s.h, 16))
        return sem.n

    def dma_t(self, eng, out, in_, sem):
        sem.n += 16
        self.q[eng].append(lambda E, o=out, i=in_, s=sem:
                           E.dma_start_transpose(out=o, in_=i).then_inc(s.h, 16))
        return sem.n

    def barrier(self):
        counts = [(s, s.n) for s in self.sems if s.n > 0]
        for eng in ENGINES:
            for s, v in counts:
                self.q[eng].append(lambda E, s=s, v=v: E.wait_ge(s.h, v))


def dap(t_ap, offset, dims):
    return bass_rust.AP(tensor=t_ap.tensor, offset=offset, ap=dims)


def sap(sb_t, offset, free_dims, nparts=128):
    h = sb_t[:].tensor
    row = int(h.shape[1])
    return bass_rust.AP(tensor=h, offset=offset,
                        ap=[[row, nparts]] + free_dims)


def build_program(cfg, n_subl=4, dump_xt=False):
    c = cfg
    NCT, NTT, NQC, NHP, NHT, NDT = c.NCT, c.NTT, c.NQC, c.NHP, c.NHT, c.NDT
    T, C, HFF, E, BT, DH, NH = c.T, c.C, c.HFF, c.E, c.BT, c.DH, c.NH
    PW = c.PW
    VROW = NH * 65
    nc = bass.Bass("TRN2", target_bir_lowering=False, debug=False)

    inp = {}

    def din(name, shape):
        inp[name] = nc.dram_tensor(name, list(shape), F32, kind="ExternalInput").ap()
        return inp[name]

    x_in = din("L0_attn_a_x", (c.B_LOC, T, C))
    for p in "qkvp":
        din(f"L0_attn_a_W{p}", (C, C)); din(f"L0_attn_a_b{p}", (C,))
    for p in "qkvo":
        din(f"L0_attn_b_W{p}", (C, C)); din(f"L0_attn_b_b{p}", (C,))
    din("L0_attn_b_rel_emb", (c.NREL, DH))
    din("L0_ffn_gate_weight", (E, C))
    din("L0_ffn_w1", (E, HFF, C)); din("L0_ffn_b1", (E, HFF))
    din("L0_ffn_w2", (E, C, HFF)); din("L0_ffn_b2", (E, C))
    out = nc.dram_tensor("out", [c.B_LOC, T, C], F32, kind="ExternalOutput").ap()

    xT_d = nc.dram_tensor("xT_d", [c.B_LOC * C * T], F32).ap()
    xnT_d = nc.dram_tensor("xnT_d", [c.B_LOC * C * T], BF16).ap()
    yT_d = nc.dram_tensor("yT_d", [C * T], BF16).ap()
    wnat, wtr = {}, {}
    for fam, ps in (("a", "qkvp"), ("b", "qkvo")):
        for p in ps:
            wnat[fam + p] = nc.dram_tensor(f"w{fam}{p}_nat", [C * C], BF16).ap()
            wtr[fam + p] = nc.dram_tensor(f"w{fam}{p}_t", [C * C], BF16).ap()
    w1_nat = nc.dram_tensor("w1_nat", [E * HFF * C], BF16).ap()
    w2_nat = nc.dram_tensor("w2_nat", [E * C * HFF], BF16).ap()
    w1_t = nc.dram_tensor("w1_t", [E * HFF * C], BF16).ap()
    w2_t = nc.dram_tensor("w2_t", [E * C * HFF], BF16).ap()
    pp_d = [nc.dram_tensor(f"pp_d{i}", [NTT * 128 * PW], BF16).ap() for i in range(2)]
    rc_d = nc.dram_tensor("rc_d", [T], BF16).ap()
    yodd_d = nc.dram_tensor("yodd_d", [64 * T], BF16).ap()
    rel_dup_d = nc.dram_tensor("rel_dup_d", [64 * PW], F32).ap()
    gts_d = nc.dram_tensor("gts_d", [E * BT], F32).ap()
    gtsb_d = nc.dram_tensor("gtsb_d", [E * BT], BF16).ap()
    lg_in = nc.dram_tensor("lg_in", [E * BT], F32).ap()
    lg_out = nc.dram_tensor("lg_out", [8 * E * BT], F32).ap()
    xt_dump = (nc.dram_tensor("xt_dump", [c.B_LOC * C * T], F32,
                              kind="ExternalOutput").ap() if dump_xt else None)

    es = contextlib.ExitStack()
    K = Emit(nc, es)
    with es:
        def sb(stack, name, shape, dt, **kw):
            return stack.enter_context(nc.sbuf_tensor(name, list(shape), dt, **kw))

        cst = sb(es, "cst", (128, 8), F32)
        ones_st = sb(es, "ones_st", (128, 1), BF16)
        ones_row = sb(es, "ones_row", (1, 128), BF16)
        onesf_st = sb(es, "onesf_st", (128, 128), F32)
        id128 = sb(es, "id128", (128, 128), BF16)
        id128f = sb(es, "id128f", (128, 128), F32)
        onetmp = sb(es, "onetmp", (128, 128), F32)
        rel_ext = sb(es, "rel_ext", (128, PW), BF16)
        rel_c = sb(es, "rel_c", (128, 2), F32)
        bias_a = sb(es, "bias_a", (128, 4 * NCT), F32)
        bias_b = sb(es, "bias_b", (128, 4 * NCT), F32)
        bklo = sb(es, "bklo", (128, NCT), F32)
        bkhi = sb(es, "bkhi", (128, NCT), F32)
        b1t = sb(es, "b1t", (128, E * NHT), F32)
        b2bf = sb(es, "b2bf", (8, C), BF16)
        gwT = sb(es, "gwT", (128, 8 * NCT), BF16)
        lgst = sb(es, "lgst", (8, 512), F32)
        bis = sb(es, "bis", (128, 8), F32)
        bcnt = sb(es, "bcnt", (128, 1), F32)
        bprd = sb(es, "bprd", (128, 1), U8)
        bprd2 = sb(es, "bprd2", (128, 1), U8)
        xtb = [sb(es, f"xtb{i}", (128, NCT * 512), F32) for i in range(2)]
        xcb = sb(es, "xcb", (128, NCT * 512), BF16)
        strow = sb(es, "strow", (1, 2 * 512), F32)
        mrbf = sb(es, "mrbf", (1, 2 * 512), BF16)

        s_pe = K.sem("s_pe"); s_dve = K.sem("s_dve"); s_act = K.sem("s_act")
        s_gp = K.sem("s_gp"); s_qk = K.sem("s_qk")
        s_prep = K.sem("s_prep")
        s_tl = [K.sem("s_tl0"), K.sem("s_tl1")]
        s_ts = [K.sem("s_ts0"), K.sem("s_ts1")]
        s_wA = K.sem("s_wA"); s_wB = K.sem("s_wB")
        s_xt = [K.sem("s_xt0"), K.sem("s_xt1")]
        s_xst = K.sem("s_xst")
        s_xn = [K.sem("s_xn0"), K.sem("s_xn1")]
        s_xns = K.sem("s_xns")
        s_pp = K.sem("s_pp"); s_sk = K.sem("s_sk")
        s_ppw = K.sem("s_ppw"); s_rg = K.sem("s_rg")
        s_h1 = K.sem("s_h1"); s_h2 = K.sem("s_h2")
        s_bc = K.sem("s_bc")
        s_io = K.sem("s_io")
        s_cc = K.sem("s_cc")

        def mm(o, l, r, start, stop, inc=None):
            K.op("tensor", lambda E, o=o, l=l, r=r, s1=start, s2=stop:
                 E.matmul(o, l, r, start=s1, stop=s2), inc=inc)

        def tpose(o, i, ident, inc=None):
            K.op("tensor", lambda E, o=o, i=i, d=ident: E.transpose(o, i, d), inc=inc)

        def act(o, i, func, bias=0.0, scale=1.0, inc=None):
            K.op("scalar", lambda E, o=o, i=i, f=func, b=bias, s=scale:
                 E.activation(o, i, f, bias=b, scale=s), inc=inc)

        def vcopy(o, i, inc=None):
            K.op("vector", lambda E, o=o, i=i: E.tensor_copy(o, i), inc=inc)

        def vtt(o, a, b, opk, inc=None):
            K.op("vector", lambda E, o=o, x=a, y=b, k=opk:
                 E.tensor_tensor(o, x, y, k), inc=inc)

        def vts(o, i, s1, op0, s2=None, op1=None, inc=None):
            if op1 is None:
                K.op("vector", lambda E, o=o, i=i, a=s1, k=op0:
                     E.tensor_scalar(o, i, a, None, k), inc=inc)
            else:
                K.op("vector", lambda E, o=o, i=i, a=s1, b=s2, k0=op0, k1=op1:
                     E.tensor_scalar(o, i, a, b, k0, k1), inc=inc)

        def vstt(o, i0, scal, i1, op0, op1, inc=None):
            K.op("vector", lambda E, o=o, i=i0, s=scal, j=i1, k0=op0, k1=op1:
                 E.scalar_tensor_tensor(o, i, s, j, k0, k1), inc=inc)

        def vset(a, v, inc=None):
            K.op("vector", lambda E, a=a, v=v: E.memset(a, v), inc=inc)

        def vred(o, i, absval=None, inc=None):
            K.op("vector", lambda E, o=o, i=i, av=absval:
                 E.tensor_reduce(o, i, axis=AX, op=ALU.add, apply_absolute_value=av),
                 inc=inc)

        def vpred(o, m, d, inc=None):
            K.op("vector", lambda E, o=o, m=m, d=d: E.copy_predicated(o, m, d), inc=inc)

        vset(cst[:, 0:1], 0.125)
        vset(cst[:, 1:2], 1.0 / C)
        vset(cst[:, 2:3], 1e-5)
        vset(cst[:, 3:4], -0.5)
        vset(cst[:, 4:5], float(c.KSEL))
        vset(cst[:, 5:6], -1.0)
        vset(cst[:, 6:7], 0.5)
        vset(ones_st[:], 1.0)
        vset(ones_row[:], 1.0)
        vset(onesf_st[:], 1.0)
        vset(onetmp[:], 1.0, inc=s_dve)
        K.wait("gpsimd", s_dve, s_dve.n)
        K.op("gpsimd", lambda E: E.affine_select(
            id128f[:], onetmp[:], pattern=[[1, 128]], compare_op=ALU.is_equal,
            fill=0.0, base=0, channel_multiplier=-1), inc=s_gp)
        K.wait("vector", s_gp, s_gp.n)
        vcopy(id128[:], id128f[:], inc=s_dve)

        esP = contextlib.ExitStack()
        with esP:
            wstage = sb(esP, "wstage", (128, C * NCT), F32)
            wstg_b = sb(esP, "wstg_b", (128, C * NCT), BF16)
            alpha = sb(esP, "alpha", (128, NCT), F32)
            ttmp = [sb(esP, f"ttmp{i}", (128, max(HFF, C)), BF16,
                       is_transpose_dest=True) for i in range(2)]
            relstage = sb(esP, "relstage", (128, 2 * DH), F32)
            relrow = sb(esP, "relrow", (1, DH), F32)
            rel_f = sb(esP, "rel_f", (128, PW), F32)
            b2f = sb(esP, "b2f", (8, C), F32)
            gwTf = sb(esP, "gwTf", (128, 8 * NCT), F32)
            xin = sb(esP, "xin", (128, C), F32)
            xout = sb(esP, "xout", (128, C), F32)
            rel_ps = esP.enter_context(nc.psum_tensor("rel_ps", [128, 384], F32))
            ti_ps = [esP.enter_context(nc.psum_tensor(f"ti_ps{i}", [128, 128], F32))
                     for i in range(2)]

            for p in "qkvo":
                K.dma("gpsimd", dap(wnat["b" + p], 0, [[1, C * C]]),
                      dap(inp[f"L0_attn_b_W{p}"], 0, [[1, C * C]]), s_prep)
            for e in range(E):
                K.dma("gpsimd", dap(w1_nat, e * HFF * C, [[1, HFF * C]]),
                      dap(inp["L0_ffn_w1"], e * HFF * C, [[1, HFF * C]]), s_prep)
                K.dma("gpsimd", dap(w2_nat, e * HFF * C, [[1, HFF * C]]),
                      dap(inp["L0_ffn_w2"], e * HFF * C, [[1, HFF * C]]), s_prep)
            prep_cast_done = s_prep.n

            for fam, btile in (("a", bias_a), ("b", bias_b)):
                for pi, p in enumerate("qkvp" if fam == "a" else "qkvo"):
                    bsrc = inp[f"L0_attn_{fam}_b{p}"]
                    for ot in range(NCT):
                        K.dma("gpsimd", btile[:, pi * NCT + ot: pi * NCT + ot + 1],
                              dap(bsrc, ot * 128, [[1, 128], [1, 1]]), s_bc)
            for e in range(E):
                for ht in range(NHT):
                    K.dma("gpsimd", b1t[:, e * NHT + ht: e * NHT + ht + 1],
                          dap(inp["L0_ffn_b1"], e * HFF + ht * 128,
                              [[1, 128], [1, 1]]), s_bc)
            K.dma("gpsimd", b2f[:], dap(inp["L0_ffn_b2"], 0, [[C, 8], [1, C]]), s_bc)
            rel = inp["L0_attn_b_rel_emb"]
            for half in range(2):
                K.dma("gpsimd", rel_c[64 * half:64 * (half + 1), 0:1],
                      dap(rel, 0, [[1, 64], [1, 1]]), s_bc)
                K.dma("gpsimd", rel_c[64 * half:64 * (half + 1), 1:2],
                      dap(rel, 2 * c.MAXREL * 64, [[1, 64], [1, 1]]), s_bc)
            for ct in range(NCT):
                K.dma("gpsimd", gwTf[:, ct * 8:(ct + 1) * 8],
                      dap(inp["L0_ffn_gate_weight"], ct * 128, [[1, 128], [C, 8]]),
                      s_bc, allow_slow_non_contiguous=True)
            K.dma("gpsimd", relstage[:, 0:DH], dap(rel, 0, [[DH, 128], [1, DH]]),
                  s_bc)
            K.dma("gpsimd", relstage[:, DH:2 * DH],
                  dap(rel, 128 * DH, [[DH, 128], [1, DH]]), s_bc)
            K.dma("gpsimd", relrow[:], dap(rel, 256 * DH, [[DH, 1], [1, DH]]), s_bc)
            smalls_dma = s_bc.n
            K.wait("vector", s_bc, smalls_dma)
            vcopy(b2bf[:], b2f[:])
            vcopy(gwT[:], gwTf[:])
            vts(bklo[:], bias_b[:, NCT:2 * NCT], rel_c[:, 0:1], ALU.add)
            vts(bkhi[:], bias_b[:, NCT:2 * NCT], rel_c[:, 1:2], ALU.add, inc=s_dve)

            K.wait("tensor", s_bc, smalls_dma)
            K.wait("tensor", s_gp, s_gp.n)
            tpose(rel_ps[0:64, 0:128], relstage[:, 0:DH], id128f[:])
            tpose(rel_ps[0:64, 128:256], relstage[:, DH:2 * DH], id128f[:])
            tpose(rel_ps[0:64, 256:257], relrow[:], id128f[0:1, 0:1], inc=s_pe)
            K.wait("vector", s_pe, s_pe.n)
            vcopy(rel_f[0:64, 127:255], rel_ps[0:64, 0:128])
            vcopy(rel_f[0:64, 255:383], rel_ps[0:64, 128:256])
            vcopy(rel_f[0:64, 383:384], rel_ps[0:64, 256:257], inc=s_dve)
            K.wait("vector", s_dve, s_dve.n)
            vts(rel_f[0:64, 0:127], onetmp[0:64, 0:127], rel_f[0:64, 127:128],
                ALU.mult)
            vts(rel_f[0:64, 384:PW], onetmp[0:64, 0:PW - 384],
                rel_f[0:64, 383:384], ALU.mult, inc=s_dve)
            K.wait("scalar", s_dve, s_dve.n)
            K.dma("scalar", dap(rel_dup_d, 0, [[PW, 64], [1, PW]]),
                  rel_f[0:64, :], s_h2)
            K.wait("scalar", s_h2, s_h2.n)
            for half in range(2):
                K.dma("scalar", rel_f[64 * half:64 * (half + 1), :],
                      dap(rel_dup_d, 0, [[PW, 64], [1, PW]]), s_h1)
            K.wait("vector", s_h1, s_h1.n)
            vcopy(rel_ext[:], rel_f[:], inc=s_dve)

            for p in "qkvp":
                w = inp[f"L0_attn_a_W{p}"]
                K.wait("scalar", s_h2, s_h2.n)
                K.wait("scalar", s_dve, s_dve.n)
                K.wait("scalar", s_act, s_act.n)
                K.dma("scalar", wstage[:].rearrange("p (n w) -> p n w", n=NCT),
                      dap(w, 0, [[C, 128], [128 * C, NCT], [1, C]]), s_h1)
                K.wait("vector", s_h1, s_h1.n)
                for ot in range(NCT):
                    vred(alpha[:, ot:ot + 1], wstage[:, ot * C:(ot + 1) * C],
                         absval=True, inc=(s_dve if ot == NCT - 1 else None))
                K.wait("vector", s_dve, s_dve.n)
                vts(alpha[:], alpha[:], 1.0 / C, ALU.mult, inc=s_dve)
                K.wait("scalar", s_dve, s_dve.n)
                for ot in range(NCT):
                    act(wstg_b[:, ot * C:(ot + 1) * C],
                        wstage[:, ot * C:(ot + 1) * C], AF.Sign,
                        inc=(s_act if ot == NCT - 1 else None))
                K.wait("vector", s_act, s_act.n)
                for ot in range(NCT):
                    vts(wstg_b[:, ot * C:(ot + 1) * C],
                        wstg_b[:, ot * C:(ot + 1) * C],
                        alpha[:, ot:ot + 1], ALU.mult,
                        inc=(s_dve if ot == NCT - 1 else None))
                K.wait("scalar", s_dve, s_dve.n)
                K.dma("scalar",
                      dap(wnat["a" + p], 0, [[C, 128], [128 * C, NCT], [1, C]]),
                      wstg_b[:].rearrange("p (n w) -> p n w", n=NCT), s_h2)
            bin_done = s_h2.n

            H1 = HFF * C // 2
            HP1 = HFF * C // 256

            def x_store(kind, dst, dst_off, ct_or_ht, slot, rows):
                v = ttmp[slot][:, 0:rows].rearrange("p (n w) -> p n w", w=128)
                if kind == "block":
                    d = dap(dst, dst_off + ct_or_ht * 128,
                            [[C, 128], [128 * C, rows // 128], [1, 128]])
                elif kind == "ctmaj":
                    d = dap(dst, dst_off + ct_or_ht * C * 128,
                            [[C, 128], [128, rows // 128], [1, 128]])
                elif kind == "w1":
                    ct = ct_or_ht
                    for h2 in range(4):
                        dd = dap(dst, dst_off + h2 * (HFF * C // 4) + ct * 128,
                                 [[HFF * C // 512, 128],
                                  [NCT * 128, NHT // 4], [1, 128]])
                        vv = sap_tt(slot, h2 * (HFF // 4), NHT // 4)
                        K.dma("sync", dd, vv, s_ts[slot])
                    return
                else:
                    ht = ct_or_ht
                    for h2 in range(4):
                        dd = dap(dst, dst_off + h2 * (C * HFF // 4) + ht * 128,
                                 [[HFF * C // 512, 128],
                                  [NHT * 128, NDT // 4], [1, 128]])
                        vv = sap_tt(slot, h2 * (C // 4), NDT // 4)
                        K.dma("sync", dd, vv, s_ts[slot])
                    return
                K.dma("sync", d, v, s_ts[slot])

            def sap_tt(slot, col0, nblk):
                return sap(ttmp[slot], col0, [[128, nblk], [1, 128]])

            def emit_wt(kind, src, src_off, src_rowlen, dst, dst_off, nblk_iter,
                        rows, wait_sem, wait_val):
                for i in range(nblk_iter):
                    slot = i % 2
                    K.wait("sync", wait_sem, wait_val)
                    K.wait("sync", s_ts[slot], s_ts[slot].n)
                    K.dma_t("sync", ttmp[slot][:, 0:rows],
                            dap(src, src_off + i * 128, [[src_rowlen, rows],
                                                         [1, 128]]), s_tl[slot])
                    K.wait("sync", s_tl[slot], s_tl[slot].n)
                    x_store(kind, dst, dst_off, i, slot, rows)

            for p, kind in (("q", "block"), ("k", "block"), ("v", "ctmaj"),
                            ("p", "block")):
                emit_wt(kind, wnat["a" + p], 0, C, wtr["a" + p], 0, NCT, C,
                        s_h2, bin_done)
            for e in range(E):
                emit_wt("w1", w1_nat, e * HFF * C, C, w1_t, e * HFF * C, NCT,
                        HFF, s_prep, prep_cast_done)
                emit_wt("w2", w2_nat, e * C * HFF, HFF, w2_t, e * C * HFF, NHT,
                        C, s_prep, prep_cast_done)
            for p, kind in (("q", "block"), ("k", "block"), ("v", "ctmaj"),
                            ("o", "block")):
                emit_wt(kind, wnat["b" + p], 0, C, wtr["b" + p], 0, NCT, C,
                        s_prep, prep_cast_done)

            for b in range(c.B_LOC):
                for tt in range(NTT):
                    K.wait("sync", s_h2, s_h2.n)
                    K.wait("sync", s_dve, s_dve.n)
                    K.dma("sync", xin[:],
                          dap(x_in, (b * T + tt * 128) * C, [[C, 128], [1, C]]),
                          s_io)
                    K.wait("tensor", s_io, s_io.n)
                    K.wait("tensor", s_dve, s_dve.n)
                    evac_at = {}
                    for ct in range(NCT):
                        pp = ti_ps[ct % 2]
                        if ct >= 2:
                            K.wait("tensor", s_dve, evac_at[ct - 2])
                        tpose(pp[:], xin[:, ct * 128:(ct + 1) * 128], id128f[:],
                              inc=s_pe)
                        K.wait("vector", s_pe, s_pe.n)
                        vcopy(xout[:, ct * 128:(ct + 1) * 128], pp[:], inc=s_dve)
                        evac_at[ct] = s_dve.n
                    K.wait("sync", s_dve, s_dve.n)
                    K.dma("sync", dap(xT_d, b * C * T + tt * 128,
                                      [[T, 128], [128 * T, NCT], [1, 128]]),
                          xout[:].rearrange("p (n w) -> p n w", n=NCT), s_h2)
            K.barrier()

        def emit_ln(b, ln_ps, do_logits, lg_ps):
            m_ps, v_ps, M_ps, R_ps = ln_ps
            for ch in range(T // 512):
                sl = ch % 2
                K.wait("gpsimd", s_dve, s_dve.n)
                K.wait("gpsimd", s_pe, s_pe.n)
                K.wait("gpsimd", s_xst, s_xst.n)
                K.dma("gpsimd", xtb[sl][:].rearrange("p (n w) -> p n w", n=NCT),
                      dap(xT_d, b * C * T + ch * 512,
                          [[T, 128], [128 * T, NCT], [1, 512]]), s_xt[sl])
                K.wait("vector", s_xt[sl], s_xt[sl].n)
                K.wait("vector", s_pe, s_pe.n)
                K.wait("vector", s_xns, s_xns.n)
                for ct in range(NCT):
                    vcopy(xcb[:, ct * 512:(ct + 1) * 512],
                          xtb[sl][:, ct * 512:(ct + 1) * 512],
                          inc=(s_dve if ct == NCT - 1 else None))
                cast_done = s_dve.n
                K.wait("tensor", s_dve, cast_done)
                for ct in range(NCT):
                    mm(m_ps, ones_st[:], xcb[:, ct * 512:(ct + 1) * 512],
                       start=(ct == 0), stop=(ct == NCT - 1),
                       inc=(s_pe if ct == NCT - 1 else None))
                m_pe = s_pe.n
                K.wait("vector", s_pe, m_pe)
                for ct in range(NCT):
                    vtt(xcb[:, ct * 512:(ct + 1) * 512],
                        xcb[:, ct * 512:(ct + 1) * 512],
                        xcb[:, ct * 512:(ct + 1) * 512], ALU.mult,
                        inc=(s_dve if ct == NCT - 1 else None))
                sq_done = s_dve.n
                K.wait("tensor", s_dve, sq_done)
                for ct in range(NCT):
                    mm(v_ps, ones_st[:], xcb[:, ct * 512:(ct + 1) * 512],
                       start=(ct == 0), stop=(ct == NCT - 1),
                       inc=(s_pe if ct == NCT - 1 else None))
                v_pe = s_pe.n
                K.wait("vector", s_pe, v_pe)
                K.wait("vector", s_act, s_act.n)
                vts(strow[:, 0:512], m_ps, cst[0:1, 1:2], ALU.mult)
                vts(strow[:, 512:1024], v_ps, cst[0:1, 1:2], ALU.mult,
                    inc=s_dve)
                K.wait("vector", s_dve, s_dve.n)
                vcopy(mrbf[:, 0:512], strow[:, 0:512], inc=s_dve)
                K.wait("vector", s_dve, s_dve.n)
                vtt(strow[:, 0:512], strow[:, 0:512], strow[:, 0:512],
                    ALU.mult, inc=s_dve)
                K.wait("vector", s_dve, s_dve.n)
                vtt(strow[:, 512:1024], strow[:, 512:1024],
                    strow[:, 0:512], ALU.subtract, inc=s_dve)
                var_done = s_dve.n
                K.wait("scalar", s_dve, var_done)
                act(strow[:, 0:512], strow[:, 512:1024], AF.Ln,
                    bias=cst[0:1, 2:3], inc=s_act)
                K.wait("scalar", s_act, s_act.n)
                act(strow[:, 512:1024], strow[:, 0:512], AF.Exp,
                    scale=cst[0:1, 3:4], inc=s_act)
                rstd_done = s_act.n
                K.wait("vector", s_act, rstd_done)
                vcopy(mrbf[:, 512:1024], strow[:, 512:1024], inc=s_dve)
                mr_done = s_dve.n
                K.wait("tensor", s_dve, mr_done)
                K.wait("tensor", s_act, s_act.n)
                mm(M_ps[:], ones_row[:], mrbf[:, 0:512], start=True, stop=True)
                mm(R_ps[:], ones_row[:], mrbf[:, 512:1024], start=True, stop=True,
                   inc=s_pe)
                bc_pe = s_pe.n
                K.wait("vector", s_pe, bc_pe)
                for ct in range(NCT):
                    vtt(xtb[sl][:, ct * 512:(ct + 1) * 512],
                        xtb[sl][:, ct * 512:(ct + 1) * 512], M_ps[:],
                        ALU.subtract, inc=(s_dve if ct == NCT - 1 else None))
                K.wait("vector", s_dve, s_dve.n)
                for ct in range(NCT):
                    vtt(xcb[:, ct * 512:(ct + 1) * 512],
                        xtb[sl][:, ct * 512:(ct + 1) * 512], R_ps[:], ALU.mult,
                        inc=(s_dve if ct == NCT - 1 else None))
                xn_done = s_dve.n
                K.wait("gpsimd", s_dve, xn_done)
                K.dma("gpsimd", dap(xnT_d, b * C * T + ch * 512,
                                    [[T, 128], [128 * T, NCT], [1, 512]]),
                      xcb[:].rearrange("p (n w) -> p n w", n=NCT), s_xns)
                if do_logits:
                    K.wait("tensor", s_dve, xn_done)
                    for ct in range(NCT):
                        mm(lg_ps[0:8, :], gwT[:, ct * 8:(ct + 1) * 8],
                           xcb[:, ct * 512:(ct + 1) * 512],
                           start=(ct == 0), stop=(ct == NCT - 1),
                           inc=(s_pe if ct == NCT - 1 else None))
                    K.wait("vector", s_pe, s_pe.n)
                    K.wait("vector", s_pp, s_pp.n)
                    vcopy(lgst[:], lg_ps[0:8, :], inc=s_dve)
                    K.wait("gpsimd", s_dve, s_dve.n)
                    K.dma("gpsimd",
                          dap(lg_in, b * T + ch * 512, [[BT, 8], [1, 512]]),
                          lgst[:], s_pp)

        def classify(kt, qc):
            d0 = 128 * kt - 512 * qc
            if d0 <= -256:
                return ("LO", None)
            if d0 >= 640:
                return ("HI", None)
            subs = []
            for qs in range(4):
                d = d0 - 128 * qs
                subs.append("LO" if d <= -256 else ("HI" if d >= 256 else "NEAR"))
            return ("MIX", subs)

        phase_ctr = [0]

        def emit_attn(kind):
            esA = contextlib.ExitStack()
            phase_ctr[0] += 1
            px = f"P{phase_ctr[0]}_"
            with esA:
                xnch = [sb(esA, px + f"at_xn{i}", (128, NCT * 512), BF16)
                        for i in range(2)]
                vps = sb(esA, px + "at_v", (128, NTT * VROW), BF16)
                qkw = [sb(esA, px + f"at_qkw{i}", (128, 2 * NCT * 128), BF16)
                       for i in range(2)]
                qts = sb(esA, px + "at_qt", (128, T), BF16)
                kts = sb(esA, px + "at_kt", (128, T), BF16)
                klo = sb(esA, px + "at_klo", (128, T), BF16) if kind == "b" else None
                khi = sb(esA, px + "at_khi", (128, T), BF16) if kind == "b" else None
                expt = sb(esA, px + "at_exp", (128, 8 * 512), BF16)
                skw = sb(esA, px + "at_skw", (128, 4 * 384), BF16) if kind == "b" else None
                ppsb = [sb(esA, px + f"at_pp{i}", (128, PW), BF16) for i in range(2)] \
                    if kind == "b" else None
                yraw = sb(esA, px + "at_yraw", (128, T), BF16)
                rcp = sb(esA, px + "at_rcp", (128, T), BF16)
                rbr = sb(esA, px + "at_rbr", (128, T), BF16)
                ytmp = sb(esA, px + "at_ytmp", (128, T), BF16)
                ytslab = sb(esA, px + "at_ytsl", (128, T), BF16)
                ymv = [sb(esA, px + f"at_ymv{i}", (128, NCT * 512), BF16)
                       for i in range(2)]
                bvb = sb(esA, px + "at_bv", (128, C), F32)
                wvv = sb(esA, px + "at_wv", (128, NCT * C), BF16)
                wpj = [sb(esA, px + f"at_wp{i}", (128, NCT * 128), BF16)
                       for i in range(2)]
                M_ps = esA.enter_context(nc.psum_tensor(px + "atp_M", [128, 512], F32))
                R_ps = esA.enter_context(nc.psum_tensor(px + "atp_R", [128, 512], F32))
                lg_ps = esA.enter_context(nc.psum_tensor(px + "atp_lg", [128, 512], F32))
                pj_ps = [esA.enter_context(
                    nc.psum_tensor(px + f"atp_pj{i}", [128, 512], F32)) for i in range(2)]
                ln_ps = (M_ps[0:1, :], R_ps[0:1, :], M_ps, R_ps)

                wq = wtr[kind + "q"]; wk = wtr[kind + "k"]
                wv = wtr[kind + "v"]
                wp = wtr[kind + ("p" if kind == "a" else "o")]
                btile = bias_a if kind == "a" else bias_b

                pj_dveg = [0, 0]
                pj_actg = [0, 0]
                bvin = inp[f"L0_attn_{kind}_bv"]
                K.dma("gpsimd", bvb[:], dap(bvin, 0, [[0, 128], [1, C]]), s_bc)
                bv_ld = s_bc.n
                K.dma("sync", wvv[:].rearrange("p (n w) -> p n w", n=NCT),
                      dap(wv, 0, [[C, 128], [128 * C, NCT], [1, C]]), s_wA)
                wv_ld = s_wA.n

                for b in range(c.B_LOC):
                    emit_ln(b, ln_ps, False, lg_ps)
                    K.wait("vector", s_pe, s_pe.n)
                    vset(sap(vps, 64, [[VROW, NTT], [65, NH]]), 1.0,
                         inc=s_dve)
                    vones = s_dve.n
                    K.wait("tensor", s_wA, wv_ld)
                    for ch in range(T // 512):
                        sl = ch % 2
                        K.wait("gpsimd", s_pe, s_pe.n)
                        K.wait("gpsimd", s_xns, s_xns.n)
                        K.dma("gpsimd",
                              xnch[sl][:].rearrange("p (n w) -> p n w", n=NCT),
                              dap(xnT_d, b * C * T + ch * 512,
                                  [[T, 128], [128 * T, NCT], [1, 512]]), s_xn[sl])
                        K.wait("tensor", s_xn[sl], s_xn[sl].n)
                        for ttl in range(4):
                            tt = ch * 4 + ttl
                            for s2 in range(C // 512):
                                bk = (ttl * (C // 512) + s2) % 2
                                pp = pj_ps[bk]
                                K.wait("tensor", s_dve, pj_dveg[bk])
                                K.wait("tensor", s_act, pj_actg[bk])
                                for ct in range(NCT):
                                    mm(pp[:],
                                       xnch[sl][:, ct * 512 + ttl * 128:
                                                ct * 512 + ttl * 128 + 128],
                                       wvv[:, ct * C + s2 * 512:
                                           ct * C + s2 * 512 + 512],
                                       start=(ct == 0), stop=(ct == NCT - 1),
                                       inc=(s_pe if ct == NCT - 1 else None))
                                K.wait("vector", s_pe, s_pe.n)
                                K.wait("vector", s_dve, vones)
                                K.wait("vector", s_bc, bv_ld)
                                vtt(sap(vps, tt * VROW + s2 * 8 * 65,
                                        [[65, 512 // 64], [1, 64]]),
                                    pp[:].rearrange("p (n w) -> p n w", n=8),
                                    bvb[:, s2 * 512:(s2 + 1) * 512]
                                    .rearrange("p (n w) -> p n w", n=8),
                                    ALU.add, inc=s_dve)
                                pj_dveg[bk] = s_dve.n
                    vdone = s_dve.n
                    sc_guard = {0: 0, 1: 0}
                    for hp in range(NHP):
                        slw = hp % 2
                        K.wait("sync", s_pe, s_pe.n)
                        K.dma("sync", qkw[slw][:, 0:NCT * 128],
                              dap(wq, hp * 128 * C, [[C, 128], [1, NCT * 128]]),
                              s_wB)
                        K.dma("sync", qkw[slw][:, NCT * 128:2 * NCT * 128],
                              dap(wk, hp * 128 * C, [[C, 128], [1, NCT * 128]]),
                              s_wB)
                        qk_ld = s_wB.n
                        K.wait("tensor", s_wB, qk_ld)
                        K.wait("tensor", s_dve, vdone)
                        for qc in range(NQC):
                            sl = qc % 2
                            K.wait("gpsimd", s_pe, s_pe.n)
                            K.wait("gpsimd", s_xns, s_xns.n)
                            K.dma("gpsimd",
                                  xnch[sl][:].rearrange("p (n w) -> p n w", n=NCT),
                                  dap(xnT_d, b * C * T + qc * 512,
                                      [[T, 128], [128 * T, NCT], [1, 512]]),
                                  s_xn[sl])
                            K.wait("tensor", s_xn[sl], s_xn[sl].n)
                            for which in (0, 1):
                                pp = pj_ps[which]
                                K.wait("tensor", s_act, pj_actg[which])
                                K.wait("tensor", s_dve, pj_dveg[which])
                                for ct in range(NCT):
                                    mm(pp[:],
                                       qkw[slw][:, which * NCT * 128 + ct * 128:
                                                which * NCT * 128 + ct * 128 + 128],
                                       xnch[sl][:, ct * 512:(ct + 1) * 512],
                                       start=(ct == 0), stop=(ct == NCT - 1),
                                       inc=(s_pe if ct == NCT - 1 else None))
                                K.wait("scalar", s_pe, s_pe.n)
                                dst = qts if which == 0 else kts
                                bcol = btile[:, which * NCT + hp:
                                             which * NCT + hp + 1]
                                act(dst[:, qc * 512:(qc + 1) * 512], pp[:],
                                    AF.Identity, bias=bcol, inc=s_act)
                                if kind == "b" and which == 1:
                                    act(klo[:, qc * 512:(qc + 1) * 512], pp[:],
                                        AF.Identity, bias=bklo[:, hp:hp + 1],
                                        inc=s_act)
                                    act(khi[:, qc * 512:(qc + 1) * 512], pp[:],
                                        AF.Identity, bias=bkhi[:, hp:hp + 1],
                                        inc=s_act)
                                pj_actg[which] = s_act.n
                        qk_act = s_act.n
                        for hi in range(2):
                            h = 2 * hp + hi
                            r0 = 64 * hi
                            K.wait("tensor", s_act, qk_act)
                            if kind == "b":
                                ppd = pp_d[hi]
                                K.wait("tensor", s_act, qk_act)
                                K.wait("sync", s_sk, s_sk.n)
                                for qt in range(NTT):
                                    pp = pj_ps[qt % 2]
                                    psl = ppsb[qt % 2]
                                    K.wait("tensor", s_dve, pj_dveg[qt % 2])
                                    K.wait("tensor", s_act, pj_actg[qt % 2])
                                    mm(pp[:, 0:PW],
                                       qts[r0:r0 + 64, qt * 128:(qt + 1) * 128],
                                       rel_ext[r0:r0 + 64, :], start=True,
                                       stop=True, inc=s_pe)
                                    K.wait("vector", s_pe, s_pe.n)
                                    K.wait("vector", s_ppw, s_ppw.n)
                                    vcopy(psl[:, 0:PW], pp[:, 0:PW], inc=s_dve)
                                    pj_dveg[qt % 2] = s_dve.n
                                    K.wait("sync", s_dve, s_dve.n)
                                    K.dma("sync",
                                          dap(ppd, qt * 128 * PW,
                                              [[PW, 128], [1, PW]]),
                                          psl[:, 0:PW], s_ppw)
                                pp_done = s_ppw.n
                            for qc in range(NQC):
                                if kind == "b":
                                    K.wait("sync", s_ppw, pp_done)
                                    K.wait("sync", s_qk, s_qk.n)
                                    for qs in range(4):
                                        qt = qc * 4 + qs
                                        K.dma("sync",
                                              skw[:, qs * 384:(qs + 1) * 384],
                                              dap(ppd, qt * 128 * PW + 127,
                                                  [[PW - 1, 128], [1, 384]]),
                                              s_sk)
                                    sk_ld = s_sk.n
                                av_ps = pj_ps[qc % 2]
                                sc_ps = [M_ps, R_ps]
                                exp_acts = {}
                                av_guard = s_pe.n
                                for kt in range(NTT):
                                    sp = sc_ps[kt % 2]
                                    K.wait("tensor", s_act, sc_guard[kt % 2])
                                    if kind == "a":
                                        case, subs = "PLAIN", None
                                    else:
                                        case, subs = classify(kt, qc)
                                    if case in ("PLAIN", "LO", "HI"):
                                        src = (kts if case == "PLAIN" else
                                               (klo if case == "LO" else khi))
                                        mm(sp[:],
                                           src[r0:r0 + 64, kt * 128:(kt + 1) * 128],
                                           qts[r0:r0 + 64, qc * 512:(qc + 1) * 512],
                                           start=True, stop=True, inc=s_qk)
                                    else:
                                        K.wait("tensor", s_sk, sk_ld)
                                        items = []
                                        for qs in range(4):
                                            q0 = qc * 512 + qs * 128
                                            sub = subs[qs]
                                            src = (klo if sub == "LO" else
                                                   khi if sub == "HI" else kts)
                                            items.append((
                                                sp[:, qs * 128:(qs + 1) * 128],
                                                src[r0:r0 + 64,
                                                    kt * 128:(kt + 1) * 128],
                                                qts[r0:r0 + 64, q0:q0 + 128]))
                                            if sub == "NEAR":
                                                off = 128 * kt - (q0 - 128)
                                                items.append((
                                                    sp[:, qs * 128:(qs + 1) * 128],
                                                    skw[:, qs * 384 + off:
                                                        qs * 384 + off + 128],
                                                    id128[:]))
                                        for ii, (o_, l_, r_) in enumerate(items):
                                            mm(o_, l_, r_, start=(ii == 0),
                                               stop=(ii == len(items) - 1),
                                               inc=(s_qk if ii == len(items) - 1
                                                    else None))
                                    K.wait("scalar", s_qk, s_qk.n)
                                    K.wait("scalar", s_pe, av_guard)
                                    act(expt[:, (kt % 8) * 512:(kt % 8) * 512 + 512],
                                        sp[:], AF.Exp, scale=cst[:, 0:1],
                                        inc=s_act)
                                    exp_acts[kt] = s_act.n
                                    sc_guard[kt % 2] = s_act.n
                                    if kt % 8 == 7 or kt == NTT - 1:
                                        K.wait("tensor", s_act, exp_acts[kt])
                                        K.wait("tensor", s_act,
                                               pj_actg[qc % 2])
                                        K.wait("tensor", s_dve, s_dve.n)
                                        for k2 in range(kt - (kt % 8), kt + 1):
                                            mm(av_ps[0:65, :],
                                               sap(vps, k2 * VROW + h * 65,
                                                   [[1, 65]]),
                                               expt[:, (k2 % 8) * 512:
                                                    (k2 % 8) * 512 + 512],
                                               start=(k2 == 0),
                                               stop=(k2 == NTT - 1),
                                               inc=(s_pe if k2 == NTT - 1
                                                    else None))
                                        av_guard = s_pe.n
                                av_pe = s_pe.n
                                K.wait("vector", s_pe, av_pe)
                                K.wait("vector", s_rg, s_rg.n)
                                vcopy(yraw[0:64, qc * 512:(qc + 1) * 512],
                                      av_ps[0:64, :], inc=s_dve)
                                pj_dveg[qc % 2] = s_dve.n
                                K.wait("scalar", s_pe, av_pe)
                                K.wait("scalar", s_act, s_act.n)
                                act(rcp[64:65, qc * 512:(qc + 1) * 512],
                                    av_ps[64:65, :], AF.Ln, inc=s_act)
                                K.wait("scalar", s_act, s_act.n)
                                act(rcp[64:65, qc * 512:(qc + 1) * 512],
                                    rcp[64:65, qc * 512:(qc + 1) * 512], AF.Exp,
                                    scale=cst[64:65, 5:6], inc=s_act)
                                pj_actg[qc % 2] = s_act.n
                            K.wait("gpsimd", s_act, s_act.n)
                            K.wait("gpsimd", s_rg, s_rg.n)
                            K.dma("gpsimd", dap(rc_d, 0, [[T, 1], [1, T]]),
                                  rcp[64:65, :], s_pp)
                            K.wait("gpsimd", s_pp, s_pp.n)
                            K.dma("gpsimd", rbr[0:64, :],
                                  dap(rc_d, 0, [[0, 64], [1, T]]), s_rg)
                            rb_ld = s_rg.n
                            K.wait("vector", s_rg, rb_ld)
                            K.wait("vector", s_dve, s_dve.n)
                            if hi == 0:
                                vtt(ytslab[0:64, :], yraw[0:64, :], rbr[0:64, :],
                                    ALU.mult, inc=s_dve)
                            else:
                                vtt(ytmp[0:64, :], yraw[0:64, :], rbr[0:64, :],
                                    ALU.mult, inc=s_dve)
                                K.wait("gpsimd", s_dve, s_dve.n)
                                K.dma("gpsimd",
                                      dap(yodd_d, 0, [[T, 64], [1, T]]),
                                      ytmp[0:64, :], s_pp)
                                K.wait("gpsimd", s_pp, s_pp.n)
                                K.dma("gpsimd", ytslab[64:128, :],
                                      dap(yodd_d, 0, [[T, 64], [1, T]]), s_rg)
                                ysl_ld = s_rg.n
                                K.wait("gpsimd", s_rg, ysl_ld)
                                K.wait("gpsimd", s_dve, s_dve.n)
                                K.dma("gpsimd",
                                      dap(yT_d, hp * 128 * T, [[T, 128], [1, T]]),
                                      ytslab[:], s_xns)
                    yt_done = s_xns.n
                    for cc in range(T // 512):
                        slx = cc % 2
                        K.wait("gpsimd", s_dve, s_dve.n)
                        K.wait("gpsimd", s_pe, s_pe.n)
                        K.wait("gpsimd", s_xst, s_xst.n)
                        K.dma("gpsimd",
                              xtb[slx][:].rearrange("p (n w) -> p n w", n=NCT),
                              dap(xT_d, b * C * T + cc * 512,
                                  [[T, 128], [128 * T, NCT], [1, 512]]),
                              s_xt[slx])
                        K.wait("gpsimd", s_xns, yt_done)
                        K.dma("gpsimd",
                              ymv[slx][:].rearrange("p (n w) -> p n w", n=NCT),
                              dap(yT_d, cc * 512,
                                  [[T, 128], [128 * T, NCT], [1, 512]]),
                              s_xn[slx])
                        for cpt in range(NCT):
                            slw2 = cpt % 2
                            K.wait("sync", s_pe, s_pe.n)
                            K.dma("sync", wpj[slw2][:],
                                  dap(wp, cpt * 128 * C,
                                      [[C, 128], [1, NCT * 128]]), s_wB)
                            K.wait("tensor", s_wB, s_wB.n)
                            K.wait("tensor", s_xn[slx], s_xn[slx].n)
                            K.wait("tensor", s_dve, pj_dveg[cpt % 2])
                            K.wait("tensor", s_act, pj_actg[cpt % 2])
                            pp = pj_ps[cpt % 2]
                            for ct in range(NCT):
                                mm(pp[:], wpj[slw2][:, ct * 128:(ct + 1) * 128],
                                   ymv[slx][:, ct * 512:(ct + 1) * 512],
                                   start=(ct == 0), stop=(ct == NCT - 1),
                                   inc=(s_pe if ct == NCT - 1 else None))
                            K.wait("vector", s_pe, s_pe.n)
                            K.wait("vector", s_xt[slx], s_xt[slx].n)
                            bcol = btile[:, 3 * NCT + cpt: 3 * NCT + cpt + 1]
                            vstt(xtb[slx][:, cpt * 512:(cpt + 1) * 512], pp[:],
                                 bcol, xtb[slx][:, cpt * 512:(cpt + 1) * 512],
                                 ALU.add, ALU.add, inc=s_dve)
                            pj_dveg[cpt % 2] = s_dve.n
                        K.wait("gpsimd", s_dve, s_dve.n)
                        K.dma("gpsimd", dap(xT_d, b * C * T + cc * 512,
                                            [[T, 128], [128 * T, NCT], [1, 512]]),
                              xtb[slx][:].rearrange("p (n w) -> p n w", n=NCT),
                              s_xst)
                K.barrier()

        def emit_ffn():
            esF = contextlib.ExitStack()
            phase_ctr[0] += 1
            px = f"P{phase_ctr[0]}_"
            with esF:
                xnu = [sb(esF, px + f"ff_xn{i}", (128, NCT * 512), BF16)
                       for i in range(2)]
                ght = sb(esF, px + "ff_gh", (128, NHT * 512), BF16)
                accs = sb(esF, px + "ff_acc", (128, NCT * 512), F32)
                gG = sb(esF, px + "ff_g", (128, 512), F32)
                gbu = [sb(esF, px + f"ff_gb{i}", (8, 512), BF16) for i in range(2)]
                glog = sb(esF, px + "ff_glog", (128, c.GL), F32)
                bisv = sb(esF, px + "ff_bisv", (128, c.GL), F32)
                gcb = sb(esF, px + "ff_gcb", (8, 2048), BF16)
                wslot = [sb(esF, px + f"ff_w{i}", (128, HFF * C // 512), BF16)
                         for i in range(2)]
                M_ps = esF.enter_context(nc.psum_tensor(px + "ffp_M", [128, 512], F32))
                R_ps = esF.enter_context(nc.psum_tensor(px + "ffp_R", [128, 512], F32))
                lg_ps = esF.enter_context(nc.psum_tensor(px + "ffp_lg", [128, 512], F32))
                hps = [esF.enter_context(
                    nc.psum_tensor(px + f"ffp_h{i}", [128, 512], F32)) for i in range(2)]
                ops = [esF.enter_context(
                    nc.psum_tensor(px + f"ffp_o{i}", [128, 512], F32)) for i in range(2)]
                cn_ps = lg_ps[0:128, 0:1]
                ln_ps = (M_ps[0:1, :], R_ps[0:1, :], M_ps, R_ps)

                for b in range(c.B_LOC):
                    emit_ln(b, ln_ps, True, lg_ps)
                K.wait("gpsimd", s_pp, s_pp.n)
                K.op("gpsimd", lambda E: E.collective_compute(
                    "AllGather", ALU.bypass, replica_groups=[list(range(8))],
                    ins=[dap(lg_in, 0, [[BT, 8], [1, BT]])],
                    outs=[dap(lg_out, 0, [[BT, 64], [1, BT]])]), inc=s_cc)
                K.wait("gpsimd", s_cc, 1 if s_cc.n <= 1 else s_cc.n)
                K.dma("gpsimd", glog[:],
                      dap(lg_out, 0, [[c.GL, 128], [1, c.GL]]), s_bc)
                gl_ld = s_bc.n
                K.wait("vector", s_bc, gl_ld)
                vset(bis[:, 0:1], -64.0)
                vset(bis[:, 1:2], 64.0, inc=s_dve)
                for it in range(c.NBISECT):
                    K.wait("vector", s_dve, s_dve.n)
                    vtt(bis[:, 2:3], bis[:, 0:1], bis[:, 1:2], ALU.add, inc=s_dve)
                    K.wait("vector", s_dve, s_dve.n)
                    vts(bis[:, 2:3], bis[:, 2:3], cst[:, 6:7], ALU.mult,
                        inc=s_dve)
                    K.wait("vector", s_dve, s_dve.n)
                    vts(bisv[:], glog[:], bis[:, 2:3], ALU.is_le, inc=s_dve)
                    K.wait("vector", s_dve, s_dve.n)
                    vred(bcnt[:], bisv[:], inc=s_dve)
                    cnt_dve = s_dve.n
                    K.wait("tensor", s_dve, cnt_dve)
                    mm(cn_ps, onesf_st[:], bcnt[:], start=True, stop=True,
                       inc=s_pe)
                    K.wait("vector", s_pe, s_pe.n)
                    vcopy(bis[:, 3:4], cn_ps, inc=s_dve)
                    K.wait("vector", s_dve, s_dve.n)
                    vts(bprd[:], bis[:, 3:4], cst[:, 4:5], ALU.is_ge)
                    vts(bprd2[:], bis[:, 3:4], cst[:, 4:5], ALU.is_lt, inc=s_dve)
                    K.wait("vector", s_dve, s_dve.n)
                    vpred(bis[:, 1:2], bprd[:], bis[:, 2:3])
                    vpred(bis[:, 0:1], bprd2[:], bis[:, 2:3], inc=s_dve)
                SEG = min(2048, BT, c.GL)
                for si in range(BT // SEG):
                    K.wait("gpsimd", s_pp, s_pp.n)
                    K.wait("gpsimd", s_dve, s_dve.n)
                    K.dma("gpsimd", glog[0:8, 0:SEG],
                          dap(lg_in, si * SEG, [[BT, 8], [1, SEG]]), s_bc)
                    K.wait("vector", s_bc, s_bc.n)
                    vts(bisv[0:8, 0:SEG], glog[0:8, 0:SEG], bis[0:8, 0:1],
                        ALU.is_gt, inc=s_dve)
                    K.wait("vector", s_dve, s_dve.n)
                    vtt(glog[0:8, 0:SEG], glog[0:8, 0:SEG],
                        bisv[0:8, 0:SEG], ALU.mult, inc=s_dve)
                    K.wait("vector", s_dve, s_dve.n)
                    vcopy(gcb[:, 0:SEG], glog[0:8, 0:SEG], inc=s_dve)
                    K.wait("gpsimd", s_dve, s_dve.n)
                    K.dma("gpsimd", dap(gts_d, si * SEG, [[BT, 8], [1, SEG]]),
                          glog[0:8, 0:SEG], s_pp)
                    K.dma("gpsimd", dap(gtsb_d, si * SEG, [[BT, 8], [1, SEG]]),
                          gcb[:, 0:SEG], s_pp)
                gates_dve = s_dve.n
                gts_done = s_pp.n

                QSZ = HFF * C // 4
                QP = HFF * C // 512
                slot_seq = []
                for u in range(c.NU):
                    for e in range(E):
                        for q in range(4):
                            slot_seq.append((w1_t, e * HFF * C + q * QSZ))
                        for q in range(4):
                            slot_seq.append((w2_t, e * C * HFF + q * QSZ))
                slot_free_pe = [0, 0]
                slot_sem = [s_wA, s_wB]
                slot_loaded = [0, 0]
                seq_i = [0]

                def issue_load():
                    i = seq_i[0]
                    if i >= len(slot_seq):
                        return
                    seq_i[0] = i + 1
                    slx = i % 2
                    srcw, off = slot_seq[i]
                    K.wait("sync", s_pe, slot_free_pe[slx])
                    K.dma("sync", wslot[slx][:],
                          dap(srcw, off, [[QP, 128], [1, QP]]), slot_sem[slx])
                    slot_loaded[slx] = slot_sem[slx].n

                issue_load(); issue_load()

                act_guard = [0, 0]
                opg = [0, 0]
                ghg = {}
                for u in range(c.NU):
                    b, ch = u // (T // 512), u % (T // 512)
                    su = u % 2
                    K.wait("gpsimd", s_pe, s_pe.n)
                    K.wait("gpsimd", s_xns, s_xns.n)
                    K.dma("gpsimd", xnu[su][:].rearrange("p (n w) -> p n w", n=NCT),
                          dap(xnT_d, b * C * T + ch * 512,
                              [[T, 128], [128 * T, NCT], [1, 512]]), s_xn[su])
                    K.wait("gpsimd", s_pp, gts_done)
                    K.dma("gpsimd", gbu[su][:],
                          dap(gtsb_d, b * T + ch * 512, [[BT, 8], [1, 512]]),
                          s_xn[su])
                    xn_ld = s_xn[su].n
                    K.wait("gpsimd", s_dve, s_dve.n)
                    K.wait("gpsimd", s_xst, s_xst.n)
                    K.dma("gpsimd", accs[:].rearrange("p (n w) -> p n w", n=NCT),
                          dap(xT_d, b * C * T + ch * 512,
                              [[T, 128], [128 * T, NCT], [1, 512]]), s_xt[0])
                    acc_ld = s_xt[0].n
                    for e in range(E):
                        K.wait("gpsimd", s_dve, s_dve.n)
                        K.wait("gpsimd", s_pp, gts_done)
                        K.dma("gpsimd", gG[:],
                              dap(gts_d, e * BT + b * T + ch * 512,
                                  [[0, 128], [1, 512]]), s_bc)
                        g_ld = s_bc.n
                        for qtr in range(4):
                            slx = (u * E * 8 + e * 8 + qtr) % 2
                            K.wait("tensor", slot_sem[slx], slot_loaded[slx])
                            if qtr == 0 and e == 0:
                                K.wait("tensor", s_xn[su], xn_ld)
                            for htl in range(NHT // 4):
                                ht = qtr * (NHT // 4) + htl
                                hpp = hps[ht % 2]
                                K.wait("tensor", s_act, act_guard[ht % 2])
                                K.wait("tensor", s_dve, ghg.get(ht, 0))
                                for ct in range(NCT):
                                    mm(hpp[:],
                                       wslot[slx][:, (htl * NCT + ct) * 128:
                                                  (htl * NCT + ct) * 128 + 128],
                                       xnu[su][:, ct * 512:(ct + 1) * 512],
                                       start=(ct == 0), stop=(ct == NCT - 1),
                                       inc=(s_pe if ct == NCT - 1 else None))
                                K.wait("scalar", s_pe, s_pe.n)
                                K.wait("scalar", s_dve, ghg.get(ht, 0))
                                act(ght[:, ht * 512:(ht + 1) * 512], hpp[:],
                                    AF.Gelu_apprx_tanh,
                                    bias=b1t[:, e * NHT + ht: e * NHT + ht + 1],
                                    inc=s_act)
                                act_guard[ht % 2] = s_act.n
                                K.wait("vector", s_act, s_act.n)
                                K.wait("vector", s_bc, g_ld)
                                vtt(ght[:, ht * 512:(ht + 1) * 512],
                                    ght[:, ht * 512:(ht + 1) * 512], gG[:],
                                    ALU.mult, inc=s_dve)
                                ghg[ht] = s_dve.n
                            slot_free_pe[slx] = s_pe.n
                            issue_load()
                        gh_done = s_dve.n
                        K.wait("tensor", s_dve, gh_done)
                        for qtr in range(4):
                            slx = (u * E * 8 + e * 8 + 4 + qtr) % 2
                            K.wait("tensor", slot_sem[slx], slot_loaded[slx])
                            for dtl in range(NDT // 4):
                                dt = qtr * (NDT // 4) + dtl
                                op = ops[dt % 2]
                                K.wait("tensor", s_dve, opg[dt % 2])
                                if e == 0:
                                    K.wait("tensor", s_xn[su], xn_ld)
                                    mm(op[:], b2bf[0:8, dt * 128:(dt + 1) * 128],
                                       gbu[su][:],
                                       start=True, stop=False)
                                for ht in range(NHT):
                                    mm(op[:],
                                       wslot[slx][:, (dtl * NHT + ht) * 128:
                                                  (dtl * NHT + ht) * 128 + 128],
                                       ght[:, ht * 512:(ht + 1) * 512],
                                       start=(ht == 0 and e != 0),
                                       stop=(ht == NHT - 1),
                                       inc=(s_pe if ht == NHT - 1 else None))
                                K.wait("vector", s_pe, s_pe.n)
                                if e == 0:
                                    K.wait("vector", s_xt[0], acc_ld)
                                vtt(accs[:, dt * 512:(dt + 1) * 512], op[:],
                                    accs[:, dt * 512:(dt + 1) * 512], ALU.add,
                                    inc=s_dve)
                                opg[dt % 2] = s_dve.n
                            slot_free_pe[slx] = s_pe.n
                            issue_load()
                    K.wait("gpsimd", s_dve, s_dve.n)
                    K.dma("gpsimd", dap(xT_d, b * C * T + ch * 512,
                                        [[T, 128], [128 * T, NCT], [1, 512]]),
                          accs[:].rearrange("p (n w) -> p n w", n=NCT), s_xst)
            K.barrier()

        plan = ["a", "f", "b", "f"][:n_subl]
        for ph in plan:
            if ph == "f":
                emit_ffn()
            else:
                emit_attn(ph)

        esO = contextlib.ExitStack()
        with esO:
            xin = sb(esO, "o_xin", (128, C), F32)
            xout = sb(esO, "o_xout", (128, C), F32)
            to_ps = [esO.enter_context(nc.psum_tensor(f"to_ps{i}", [128, 128], F32))
                     for i in range(2)]
            for b in range(c.B_LOC):
                for tt in range(NTT):
                    K.wait("sync", s_io, s_io.n)
                    K.wait("sync", s_dve, s_dve.n)
                    K.wait("sync", s_xst, s_xst.n)
                    K.dma("sync", xin[:].rearrange("p (n w) -> p n w", n=NCT),
                          dap(xT_d, b * C * T + tt * 128,
                              [[T, 128], [128 * T, NCT], [1, 128]]), s_h1)
                    K.wait("tensor", s_h1, s_h1.n)
                    K.wait("tensor", s_dve, s_dve.n)
                    evac_at = {}
                    for ct in range(NCT):
                        pp = to_ps[ct % 2]
                        if ct >= 2:
                            K.wait("tensor", s_dve, evac_at[ct - 2])
                        tpose(pp[:], xin[:, ct * 128:(ct + 1) * 128], id128f[:],
                              inc=s_pe)
                        K.wait("vector", s_pe, s_pe.n)
                        vcopy(xout[:, ct * 128:(ct + 1) * 128], pp[:], inc=s_dve)
                        evac_at[ct] = s_dve.n
                    K.wait("sync", s_dve, s_dve.n)
                    K.dma("sync",
                          dap(out, (b * T + tt * 128) * C, [[C, 128], [1, C]]),
                          xout[:], s_io)
            if dump_xt:
                K.wait("sync", s_xst, s_xst.n)
                K.dma("sync", dap(xt_dump, 0, [[1, c.B_LOC * C * T]]),
                      dap(xT_d, 0, [[1, c.B_LOC * C * T]]), s_io)
            K.wait("sync", s_io, s_io.n)

        with nc.Block() as block:
            @block.sync
            def _(E):
                for f in K.q["sync"]:
                    f(E)

            @block.tensor
            def _(E):
                for f in K.q["tensor"]:
                    f(E)

            @block.vector
            def _(E):
                for f in K.q["vector"]:
                    f(E)

            @block.scalar
            def _(E):
                for f in K.q["scalar"]:
                    f(E)

            @block.gpsimd
            def _(E):
                for f in K.q["gpsimd"]:
                    f(E)
    return nc


def kernel(**inputs):
    cfg = Cfg(mini=False)
    nc = build_program(cfg, n_subl=4)
    x = np.ascontiguousarray(inputs["L0_attn_a_x"])
    shards = np.split(x, 8, axis=0)
    base = {k: np.ascontiguousarray(v) for k, v in inputs.items()
            if k != "L0_attn_a_x"}
    in_maps = [dict(base, L0_attn_a_x=shards[i]) for i in range(8)]
    res = run_bass_kernel_spmd(nc, in_maps, core_ids=list(range(8)))
    return np.concatenate([res.results[i]["out"] for i in range(8)], axis=0)


# version 06
# speedup vs baseline: 1.2413x; 1.0471x over previous
import contextlib
import numpy as np

import bass_rust
import concourse.bass as bass
import concourse.mybir as mybir
from concourse.bass_utils import run_bass_kernel_spmd

F32 = mybir.dt.float32
BF16 = mybir.dt.bfloat16
U8 = mybir.dt.uint8
AF = mybir.ActivationFunctionType
ALU = mybir.AluOpType
AX = mybir.AxisListType.X

ENGINES = ("sync", "tensor", "vector", "scalar", "gpsimd")


class Cfg:
    def __init__(self, mini=False):
        if mini == "midi":
            self.B_LOC, self.T, self.C, self.HFF, self.E = 2, 1536, 512, 1024, 8
            self.KSEL, self.NBISECT = 4608, 40
        elif mini:
            self.B_LOC, self.T, self.C, self.HFF, self.E = 2, 512, 512, 1024, 8
            self.KSEL, self.NBISECT = 6144, 40
        else:
            self.B_LOC, self.T, self.C, self.HFF, self.E = 2, 2048, 1024, 4096, 8
            self.KSEL, self.NBISECT = 24576, 72
        self.DH = 64
        self.NH = self.C // self.DH
        self.NHP = self.NH // 2
        self.NCT = self.C // 128
        self.NTT = self.T // 128
        self.NQC = self.T // 512
        self.NHT = self.HFF // 128
        self.NDT = self.C // 128
        self.MAXREL = 128
        self.NREL = 2 * self.MAXREL + 1
        self.PW = 511
        self.BT = self.B_LOC * self.T
        self.GTOT = 8 * self.BT * self.E
        self.GL = self.GTOT // 128
        self.NU = self.BT // 512


class Sem:
    __slots__ = ("h", "n", "name")

    def __init__(self, h, name):
        self.h, self.n, self.name = h, 0, name


class Emit:
    def __init__(self, nc, es):
        self.nc, self.es = nc, es
        self.q = {e: [] for e in ENGINES}
        self.sems = []

    def sem(self, name):
        s = Sem(self.es.enter_context(self.nc.semaphore(name)), name)
        self.sems.append(s)
        return s

    def op(self, eng, fn, inc=None, inc_by=1):
        if inc is None:
            self.q[eng].append(lambda E, fn=fn: fn(E))
            return None
        inc.n += inc_by
        self.q[eng].append(lambda E, fn=fn, s=inc, b=inc_by: fn(E).then_inc(s.h, b))
        return inc.n

    def wait(self, eng, sem, val=None):
        v = sem.n if val is None else val
        if v > 0:
            self.q[eng].append(lambda E, s=sem, v=v: E.wait_ge(s.h, v))
        return v

    def dma(self, eng, out, in_, sem, **kw):
        sem.n += 16
        self.q[eng].append(lambda E, o=out, i=in_, s=sem, kw=kw:
                           E.dma_start(out=o, in_=i, **kw).then_inc(s.h, 16))
        return sem.n

    def dma_t(self, eng, out, in_, sem):
        sem.n += 16
        self.q[eng].append(lambda E, o=out, i=in_, s=sem:
                           E.dma_start_transpose(out=o, in_=i).then_inc(s.h, 16))
        return sem.n

    def barrier(self):
        counts = [(s, s.n) for s in self.sems if s.n > 0]
        for eng in ENGINES:
            for s, v in counts:
                self.q[eng].append(lambda E, s=s, v=v: E.wait_ge(s.h, v))


def dap(t_ap, offset, dims):
    return bass_rust.AP(tensor=t_ap.tensor, offset=offset, ap=dims)


def sap(sb_t, offset, free_dims, nparts=128):
    h = sb_t[:].tensor
    row = int(h.shape[1])
    return bass_rust.AP(tensor=h, offset=offset,
                        ap=[[row, nparts]] + free_dims)


def build_program(cfg, n_subl=4, dump_xt=False):
    c = cfg
    NCT, NTT, NQC, NHP, NHT, NDT = c.NCT, c.NTT, c.NQC, c.NHP, c.NHT, c.NDT
    T, C, HFF, E, BT, DH, NH = c.T, c.C, c.HFF, c.E, c.BT, c.DH, c.NH
    PW = c.PW
    VROW = NH * 65
    nc = bass.Bass("TRN2", target_bir_lowering=False, debug=False)

    inp = {}

    def din(name, shape):
        inp[name] = nc.dram_tensor(name, list(shape), F32, kind="ExternalInput").ap()
        return inp[name]

    x_in = din("L0_attn_a_x", (c.B_LOC, T, C))
    for p in "qkvp":
        din(f"L0_attn_a_W{p}", (C, C)); din(f"L0_attn_a_b{p}", (C,))
    for p in "qkvo":
        din(f"L0_attn_b_W{p}", (C, C)); din(f"L0_attn_b_b{p}", (C,))
    din("L0_attn_b_rel_emb", (c.NREL, DH))
    din("L0_ffn_gate_weight", (E, C))
    din("L0_ffn_w1", (E, HFF, C)); din("L0_ffn_b1", (E, HFF))
    din("L0_ffn_w2", (E, C, HFF)); din("L0_ffn_b2", (E, C))
    out = nc.dram_tensor("out", [c.B_LOC, T, C], F32, kind="ExternalOutput").ap()

    xT_d = nc.dram_tensor("xT_d", [c.B_LOC * C * T], F32).ap()
    xnT_d = nc.dram_tensor("xnT_d", [c.B_LOC * C * T], BF16).ap()
    yT_d = nc.dram_tensor("yT_d", [C * T], BF16).ap()
    wnat, wtr = {}, {}
    for fam, ps in (("a", "qkvp"), ("b", "qkvo")):
        for p in ps:
            wnat[fam + p] = nc.dram_tensor(f"w{fam}{p}_nat", [C * C], BF16).ap()
            wtr[fam + p] = nc.dram_tensor(f"w{fam}{p}_t", [C * C], BF16).ap()
    w1_nat = nc.dram_tensor("w1_nat", [E * HFF * C], BF16).ap()
    w2_nat = nc.dram_tensor("w2_nat", [E * C * HFF], BF16).ap()
    w1_t2 = nc.dram_tensor("w1_t2", [E * HFF * C], BF16).ap()
    w2_t = nc.dram_tensor("w2_t", [E * C * HFF], BF16).ap()
    pp_d = [nc.dram_tensor(f"pp_d{i}", [NTT * 128 * PW], BF16).ap() for i in range(2)]
    rc_d = nc.dram_tensor("rc_d", [T], BF16).ap()
    yodd_d = nc.dram_tensor("yodd_d", [64 * T], BF16).ap()
    rel_dup_d = nc.dram_tensor("rel_dup_d", [64 * PW], F32).ap()
    gts_d = nc.dram_tensor("gts_d", [E * BT], F32).ap()
    gtsb_d = nc.dram_tensor("gtsb_d", [E * BT], BF16).ap()
    lg_in = nc.dram_tensor("lg_in", [E * BT], F32).ap()
    lg_out = nc.dram_tensor("lg_out", [8 * E * BT], F32).ap()
    xt_dump = (nc.dram_tensor("xt_dump", [c.B_LOC * C * T], F32,
                              kind="ExternalOutput").ap() if dump_xt else None)

    es = contextlib.ExitStack()
    K = Emit(nc, es)
    with es:
        def sb(stack, name, shape, dt, **kw):
            return stack.enter_context(nc.sbuf_tensor(name, list(shape), dt, **kw))

        cst = sb(es, "cst", (128, 8), F32)
        ones_st = sb(es, "ones_st", (128, 1), BF16)
        ones_row = sb(es, "ones_row", (1, 128), BF16)
        onesf_st = sb(es, "onesf_st", (128, 128), F32)
        id128 = sb(es, "id128", (128, 128), BF16)
        id128f = sb(es, "id128f", (128, 128), F32)
        onetmp = sb(es, "onetmp", (128, 128), F32)
        rel_ext = sb(es, "rel_ext", (128, PW), BF16)
        rel_c = sb(es, "rel_c", (128, 2), F32)
        bias_a = sb(es, "bias_a", (128, 4 * NCT), F32)
        bias_b = sb(es, "bias_b", (128, 4 * NCT), F32)
        bklo = sb(es, "bklo", (128, NCT), F32)
        bkhi = sb(es, "bkhi", (128, NCT), F32)
        b1t = sb(es, "b1t", (128, E * NHT), F32)
        b2bf = sb(es, "b2bf", (8, C), BF16)
        gwT = sb(es, "gwT", (128, 8 * NCT), BF16)
        lgst = sb(es, "lgst", (8, 512), F32)
        bis = sb(es, "bis", (128, 8), F32)
        bcnt = sb(es, "bcnt", (128, 1), F32)
        bprd = sb(es, "bprd", (128, 1), U8)
        bprd2 = sb(es, "bprd2", (128, 1), U8)
        xtb = [sb(es, f"xtb{i}", (128, NCT * 512), F32) for i in range(2)]
        xcb = sb(es, "xcb", (128, NCT * 512), BF16)
        esW = contextlib.ExitStack()
        ttmpw = [sb(esW, f"ttmpw{i}", (128, max(HFF, C)), BF16,
                    is_transpose_dest=True, side="right") for i in range(2)]
        strow = sb(es, "strow", (1, 2 * 512), F32)
        mrbf = sb(es, "mrbf", (1, 2 * 512), BF16)

        s_pe = K.sem("s_pe"); s_dve = K.sem("s_dve"); s_act = K.sem("s_act")
        s_gp = K.sem("s_gp"); s_qk = K.sem("s_qk")
        s_prep = K.sem("s_prep")
        s_tl = [K.sem("s_tl0"), K.sem("s_tl1")]
        s_ts = [K.sem("s_ts0"), K.sem("s_ts1")]
        s_wA = K.sem("s_wA"); s_wB = K.sem("s_wB")
        s_xt = [K.sem("s_xt0"), K.sem("s_xt1")]
        s_xst = K.sem("s_xst")
        s_xn = [K.sem("s_xn0"), K.sem("s_xn1")]
        s_xns = K.sem("s_xns")
        s_pp = K.sem("s_pp"); s_sk = K.sem("s_sk")
        s_ppw = K.sem("s_ppw"); s_rg = K.sem("s_rg")
        s_h1 = K.sem("s_h1"); s_h2 = K.sem("s_h2")
        s_bc = K.sem("s_bc")
        s_io = K.sem("s_io")
        s_cc = K.sem("s_cc")

        def mm(o, l, r, start, stop, inc=None):
            K.op("tensor", lambda E, o=o, l=l, r=r, s1=start, s2=stop:
                 E.matmul(o, l, r, start=s1, stop=s2), inc=inc)

        def tpose(o, i, ident, inc=None):
            K.op("tensor", lambda E, o=o, i=i, d=ident: E.transpose(o, i, d), inc=inc)

        def act(o, i, func, bias=0.0, scale=1.0, inc=None):
            K.op("scalar", lambda E, o=o, i=i, f=func, b=bias, s=scale:
                 E.activation(o, i, f, bias=b, scale=s), inc=inc)

        def vcopy(o, i, inc=None):
            K.op("vector", lambda E, o=o, i=i: E.tensor_copy(o, i), inc=inc)

        def vtt(o, a, b, opk, inc=None):
            K.op("vector", lambda E, o=o, x=a, y=b, k=opk:
                 E.tensor_tensor(o, x, y, k), inc=inc)

        def vts(o, i, s1, op0, s2=None, op1=None, inc=None):
            if op1 is None:
                K.op("vector", lambda E, o=o, i=i, a=s1, k=op0:
                     E.tensor_scalar(o, i, a, None, k), inc=inc)
            else:
                K.op("vector", lambda E, o=o, i=i, a=s1, b=s2, k0=op0, k1=op1:
                     E.tensor_scalar(o, i, a, b, k0, k1), inc=inc)

        def vstt(o, i0, scal, i1, op0, op1, inc=None):
            K.op("vector", lambda E, o=o, i=i0, s=scal, j=i1, k0=op0, k1=op1:
                 E.scalar_tensor_tensor(o, i, s, j, k0, k1), inc=inc)

        def vset(a, v, inc=None):
            K.op("vector", lambda E, a=a, v=v: E.memset(a, v), inc=inc)

        def vred(o, i, absval=None, inc=None):
            K.op("vector", lambda E, o=o, i=i, av=absval:
                 E.tensor_reduce(o, i, axis=AX, op=ALU.add, apply_absolute_value=av),
                 inc=inc)

        def vpred(o, m, d, inc=None):
            K.op("vector", lambda E, o=o, m=m, d=d: E.copy_predicated(o, m, d), inc=inc)

        vset(cst[:, 0:1], 0.125)
        vset(cst[:, 1:2], 1.0 / C)
        vset(cst[:, 2:3], 1e-5)
        vset(cst[:, 3:4], -0.5)
        vset(cst[:, 4:5], float(c.KSEL))
        vset(cst[:, 5:6], -1.0)
        vset(cst[:, 6:7], 0.5)
        vset(ones_st[:], 1.0)
        vset(ones_row[:], 1.0)
        vset(onesf_st[:], 1.0)
        vset(onetmp[:], 1.0, inc=s_dve)
        K.wait("gpsimd", s_dve, s_dve.n)
        K.op("gpsimd", lambda E: E.affine_select(
            id128f[:], onetmp[:], pattern=[[1, 128]], compare_op=ALU.is_equal,
            fill=0.0, base=0, channel_multiplier=-1), inc=s_gp)
        K.wait("vector", s_gp, s_gp.n)
        vcopy(id128[:], id128f[:], inc=s_dve)

        esP = contextlib.ExitStack()
        with esP:
            wstage = sb(esP, "wstage", (128, C * NCT), F32)
            wstg_b = sb(esP, "wstg_b", (128, C * NCT), BF16)
            alpha = sb(esP, "alpha", (128, NCT), F32)
            ttmp = [sb(esP, f"ttmp{i}", (128, max(HFF, C)), BF16,
                       is_transpose_dest=True) for i in range(2)]
            relstage = sb(esP, "relstage", (128, 2 * DH), F32)
            relrow = sb(esP, "relrow", (1, DH), F32)
            rel_f = sb(esP, "rel_f", (128, PW), F32)
            b2f = sb(esP, "b2f", (8, C), F32)
            gwTf = sb(esP, "gwTf", (128, 8 * NCT), F32)
            xin = sb(esP, "xin", (128, C), F32)
            xout = sb(esP, "xout", (128, C), F32)
            rel_ps = esP.enter_context(nc.psum_tensor("rel_ps", [128, 384], F32))
            ti_ps = [esP.enter_context(nc.psum_tensor(f"ti_ps{i}", [128, 128], F32))
                     for i in range(2)]

            for p in "qkvo":
                K.dma("gpsimd", dap(wnat["b" + p], 0, [[1, C * C]]),
                      dap(inp[f"L0_attn_b_W{p}"], 0, [[1, C * C]]), s_prep)
            cast_w1, cast_w2 = {}, {}
            for e in range(E):
                cast_w1[e] = K.dma("gpsimd", dap(w1_nat, e * HFF * C, [[1, HFF * C]]),
                      dap(inp["L0_ffn_w1"], e * HFF * C, [[1, HFF * C]]), s_prep)
                cast_w2[e] = K.dma("gpsimd", dap(w2_nat, e * HFF * C, [[1, HFF * C]]),
                      dap(inp["L0_ffn_w2"], e * HFF * C, [[1, HFF * C]]), s_prep)
            prep_cast_done = s_prep.n

            for fam, btile in (("a", bias_a), ("b", bias_b)):
                for pi, p in enumerate("qkvp" if fam == "a" else "qkvo"):
                    bsrc = inp[f"L0_attn_{fam}_b{p}"]
                    for ot in range(NCT):
                        K.dma("gpsimd", btile[:, pi * NCT + ot: pi * NCT + ot + 1],
                              dap(bsrc, ot * 128, [[1, 128], [1, 1]]), s_bc)
            for e in range(E):
                for ht in range(NHT):
                    K.dma("gpsimd", b1t[:, e * NHT + ht: e * NHT + ht + 1],
                          dap(inp["L0_ffn_b1"], e * HFF + ht * 128,
                              [[1, 128], [1, 1]]), s_bc)
            K.dma("gpsimd", b2f[:], dap(inp["L0_ffn_b2"], 0, [[C, 8], [1, C]]), s_bc)
            rel = inp["L0_attn_b_rel_emb"]
            for half in range(2):
                K.dma("gpsimd", rel_c[64 * half:64 * (half + 1), 0:1],
                      dap(rel, 0, [[1, 64], [1, 1]]), s_bc)
                K.dma("gpsimd", rel_c[64 * half:64 * (half + 1), 1:2],
                      dap(rel, 2 * c.MAXREL * 64, [[1, 64], [1, 1]]), s_bc)
            for ct in range(NCT):
                K.dma("gpsimd", gwTf[:, ct * 8:(ct + 1) * 8],
                      dap(inp["L0_ffn_gate_weight"], ct * 128, [[1, 128], [C, 8]]),
                      s_bc, allow_slow_non_contiguous=True)
            K.dma("gpsimd", relstage[:, 0:DH], dap(rel, 0, [[DH, 128], [1, DH]]),
                  s_bc)
            K.dma("gpsimd", relstage[:, DH:2 * DH],
                  dap(rel, 128 * DH, [[DH, 128], [1, DH]]), s_bc)
            K.dma("gpsimd", relrow[:], dap(rel, 256 * DH, [[DH, 1], [1, DH]]), s_bc)
            smalls_dma = s_bc.n
            K.wait("vector", s_bc, smalls_dma)
            vcopy(b2bf[:], b2f[:])
            vcopy(gwT[:], gwTf[:])
            vts(bklo[:], bias_b[:, NCT:2 * NCT], rel_c[:, 0:1], ALU.add)
            vts(bkhi[:], bias_b[:, NCT:2 * NCT], rel_c[:, 1:2], ALU.add, inc=s_dve)

            K.wait("tensor", s_bc, smalls_dma)
            K.wait("tensor", s_gp, s_gp.n)
            tpose(rel_ps[0:64, 0:128], relstage[:, 0:DH], id128f[:])
            tpose(rel_ps[0:64, 128:256], relstage[:, DH:2 * DH], id128f[:])
            tpose(rel_ps[0:64, 256:257], relrow[:], id128f[0:1, 0:1], inc=s_pe)
            K.wait("vector", s_pe, s_pe.n)
            vcopy(rel_f[0:64, 127:255], rel_ps[0:64, 0:128])
            vcopy(rel_f[0:64, 255:383], rel_ps[0:64, 128:256])
            vcopy(rel_f[0:64, 383:384], rel_ps[0:64, 256:257], inc=s_dve)
            K.wait("vector", s_dve, s_dve.n)
            vts(rel_f[0:64, 0:127], onetmp[0:64, 0:127], rel_f[0:64, 127:128],
                ALU.mult)
            vts(rel_f[0:64, 384:PW], onetmp[0:64, 0:PW - 384],
                rel_f[0:64, 383:384], ALU.mult, inc=s_dve)
            K.wait("scalar", s_dve, s_dve.n)
            K.dma("scalar", dap(rel_dup_d, 0, [[PW, 64], [1, PW]]),
                  rel_f[0:64, :], s_h2)
            K.wait("scalar", s_h2, s_h2.n)
            for half in range(2):
                K.dma("scalar", rel_f[64 * half:64 * (half + 1), :],
                      dap(rel_dup_d, 0, [[PW, 64], [1, PW]]), s_h1)
            K.wait("vector", s_h1, s_h1.n)
            vcopy(rel_ext[:], rel_f[:], inc=s_dve)

            for p in "qkvp":
                w = inp[f"L0_attn_a_W{p}"]
                K.wait("scalar", s_h2, s_h2.n)
                K.wait("scalar", s_dve, s_dve.n)
                K.wait("scalar", s_act, s_act.n)
                K.dma("scalar", wstage[:].rearrange("p (n w) -> p n w", n=NCT),
                      dap(w, 0, [[C, 128], [128 * C, NCT], [1, C]]), s_h1)
                K.wait("vector", s_h1, s_h1.n)
                for ot in range(NCT):
                    vred(alpha[:, ot:ot + 1], wstage[:, ot * C:(ot + 1) * C],
                         absval=True, inc=(s_dve if ot == NCT - 1 else None))
                K.wait("vector", s_dve, s_dve.n)
                vts(alpha[:], alpha[:], 1.0 / C, ALU.mult, inc=s_dve)
                K.wait("scalar", s_dve, s_dve.n)
                for ot in range(NCT):
                    act(wstg_b[:, ot * C:(ot + 1) * C],
                        wstage[:, ot * C:(ot + 1) * C], AF.Sign,
                        inc=(s_act if ot == NCT - 1 else None))
                K.wait("vector", s_act, s_act.n)
                for ot in range(NCT):
                    vts(wstg_b[:, ot * C:(ot + 1) * C],
                        wstg_b[:, ot * C:(ot + 1) * C],
                        alpha[:, ot:ot + 1], ALU.mult,
                        inc=(s_dve if ot == NCT - 1 else None))
                K.wait("scalar", s_dve, s_dve.n)
                K.dma("scalar",
                      dap(wnat["a" + p], 0, [[C, 128], [128 * C, NCT], [1, C]]),
                      wstg_b[:].rearrange("p (n w) -> p n w", n=NCT), s_h2)
            bin_done = s_h2.n

            def x_store(kind, dst, dst_off, ct, slot):
                v = ttmp[slot][:, 0:C].rearrange("p (n w) -> p n w", w=128)
                if kind == "block":
                    d = dap(dst, dst_off + ct * 128,
                            [[C, 128], [128 * C, NCT], [1, 128]])
                else:
                    d = dap(dst, dst_off + ct * C * 128,
                            [[C, 128], [128, NCT], [1, 128]])
                K.dma("sync", d, v, s_ts[slot])

            def emit_wt(kind, src, src_off, dst, dst_off, wait_sem, wait_val):
                for i in range(NCT):
                    slot = i % 2
                    K.wait("sync", wait_sem, wait_val)
                    K.wait("sync", s_ts[slot], s_ts[slot].n)
                    K.dma_t("sync", ttmp[slot][:, 0:C],
                            dap(src, src_off + i * 128, [[C, C], [1, 128]]),
                            s_tl[slot])
                    K.wait("sync", s_tl[slot], s_tl[slot].n)
                    x_store(kind, dst, dst_off, i, slot)

            for p, kind in (("q", "block"), ("k", "block"), ("v", "ctmaj"),
                            ("p", "block")):
                emit_wt(kind, wnat["a" + p], 0, wtr["a" + p], 0, s_h2, bin_done)
            for p, kind in (("q", "block"), ("k", "block"), ("v", "ctmaj"),
                            ("o", "block")):
                emit_wt(kind, wnat["b" + p], 0, wtr["b" + p], 0,
                        s_prep, prep_cast_done)

            for b in range(c.B_LOC):
                for tt in range(NTT):
                    K.wait("sync", s_h2, s_h2.n)
                    K.wait("sync", s_dve, s_dve.n)
                    K.dma("sync", xin[:],
                          dap(x_in, (b * T + tt * 128) * C, [[C, 128], [1, C]]),
                          s_io)
                    K.wait("tensor", s_io, s_io.n)
                    K.wait("tensor", s_dve, s_dve.n)
                    evac_at = {}
                    for ct in range(NCT):
                        pp = ti_ps[ct % 2]
                        if ct >= 2:
                            K.wait("tensor", s_dve, evac_at[ct - 2])
                        tpose(pp[:], xin[:, ct * 128:(ct + 1) * 128], id128f[:],
                              inc=s_pe)
                        K.wait("vector", s_pe, s_pe.n)
                        vcopy(xout[:, ct * 128:(ct + 1) * 128], pp[:], inc=s_dve)
                        evac_at[ct] = s_dve.n
                    K.wait("sync", s_dve, s_dve.n)
                    K.dma("sync", dap(xT_d, b * C * T + tt * 128,
                                      [[T, 128], [128 * T, NCT], [1, 128]]),
                          xout[:].rearrange("p (n w) -> p n w", n=NCT), s_h2)
            K.barrier()

        def emit_ln(b, ln_ps, do_logits, lg_ps):
            m_ps, v_ps, M_ps, R_ps = ln_ps
            for ch in range(T // 512):
                sl = ch % 2
                K.wait("gpsimd", s_dve, s_dve.n)
                K.wait("gpsimd", s_pe, s_pe.n)
                K.wait("gpsimd", s_xst, s_xst.n)
                K.dma("gpsimd", xtb[sl][:].rearrange("p (n w) -> p n w", n=NCT),
                      dap(xT_d, b * C * T + ch * 512,
                          [[T, 128], [128 * T, NCT], [1, 512]]), s_xt[sl])
                K.wait("vector", s_xt[sl], s_xt[sl].n)
                K.wait("vector", s_pe, s_pe.n)
                K.wait("vector", s_xns, s_xns.n)
                for ct in range(NCT):
                    vcopy(xcb[:, ct * 512:(ct + 1) * 512],
                          xtb[sl][:, ct * 512:(ct + 1) * 512],
                          inc=(s_dve if ct == NCT - 1 else None))
                cast_done = s_dve.n
                K.wait("tensor", s_dve, cast_done)
                for ct in range(NCT):
                    mm(m_ps, ones_st[:], xcb[:, ct * 512:(ct + 1) * 512],
                       start=(ct == 0), stop=(ct == NCT - 1),
                       inc=(s_pe if ct == NCT - 1 else None))
                m_pe = s_pe.n
                K.wait("vector", s_pe, m_pe)
                for ct in range(NCT):
                    vtt(xcb[:, ct * 512:(ct + 1) * 512],
                        xcb[:, ct * 512:(ct + 1) * 512],
                        xcb[:, ct * 512:(ct + 1) * 512], ALU.mult,
                        inc=(s_dve if ct == NCT - 1 else None))
                sq_done = s_dve.n
                K.wait("tensor", s_dve, sq_done)
                for ct in range(NCT):
                    mm(v_ps, ones_st[:], xcb[:, ct * 512:(ct + 1) * 512],
                       start=(ct == 0), stop=(ct == NCT - 1),
                       inc=(s_pe if ct == NCT - 1 else None))
                v_pe = s_pe.n
                K.wait("vector", s_pe, v_pe)
                K.wait("vector", s_act, s_act.n)
                vts(strow[:, 0:512], m_ps, cst[0:1, 1:2], ALU.mult)
                vts(strow[:, 512:1024], v_ps, cst[0:1, 1:2], ALU.mult,
                    inc=s_dve)
                K.wait("vector", s_dve, s_dve.n)
                vcopy(mrbf[:, 0:512], strow[:, 0:512], inc=s_dve)
                K.wait("vector", s_dve, s_dve.n)
                vtt(strow[:, 0:512], strow[:, 0:512], strow[:, 0:512],
                    ALU.mult, inc=s_dve)
                K.wait("vector", s_dve, s_dve.n)
                vtt(strow[:, 512:1024], strow[:, 512:1024],
                    strow[:, 0:512], ALU.subtract, inc=s_dve)
                var_done = s_dve.n
                K.wait("scalar", s_dve, var_done)
                act(strow[:, 0:512], strow[:, 512:1024], AF.Ln,
                    bias=cst[0:1, 2:3], inc=s_act)
                K.wait("scalar", s_act, s_act.n)
                act(strow[:, 512:1024], strow[:, 0:512], AF.Exp,
                    scale=cst[0:1, 3:4], inc=s_act)
                rstd_done = s_act.n
                K.wait("vector", s_act, rstd_done)
                vcopy(mrbf[:, 512:1024], strow[:, 512:1024], inc=s_dve)
                mr_done = s_dve.n
                K.wait("tensor", s_dve, mr_done)
                K.wait("tensor", s_act, s_act.n)
                mm(M_ps[:], ones_row[:], mrbf[:, 0:512], start=True, stop=True)
                mm(R_ps[:], ones_row[:], mrbf[:, 512:1024], start=True, stop=True,
                   inc=s_pe)
                bc_pe = s_pe.n
                K.wait("vector", s_pe, bc_pe)
                for ct in range(NCT):
                    vtt(xtb[sl][:, ct * 512:(ct + 1) * 512],
                        xtb[sl][:, ct * 512:(ct + 1) * 512], M_ps[:],
                        ALU.subtract, inc=(s_dve if ct == NCT - 1 else None))
                K.wait("vector", s_dve, s_dve.n)
                for ct in range(NCT):
                    vtt(xcb[:, ct * 512:(ct + 1) * 512],
                        xtb[sl][:, ct * 512:(ct + 1) * 512], R_ps[:], ALU.mult,
                        inc=(s_dve if ct == NCT - 1 else None))
                xn_done = s_dve.n
                K.wait("gpsimd", s_dve, xn_done)
                K.dma("gpsimd", dap(xnT_d, b * C * T + ch * 512,
                                    [[T, 128], [128 * T, NCT], [1, 512]]),
                      xcb[:].rearrange("p (n w) -> p n w", n=NCT), s_xns)
                if do_logits:
                    K.wait("tensor", s_dve, xn_done)
                    for ct in range(NCT):
                        mm(lg_ps[0:8, :], gwT[:, ct * 8:(ct + 1) * 8],
                           xcb[:, ct * 512:(ct + 1) * 512],
                           start=(ct == 0), stop=(ct == NCT - 1),
                           inc=(s_pe if ct == NCT - 1 else None))
                    K.wait("vector", s_pe, s_pe.n)
                    K.wait("vector", s_pp, s_pp.n)
                    vcopy(lgst[:], lg_ps[0:8, :], inc=s_dve)
                    K.wait("gpsimd", s_dve, s_dve.n)
                    K.dma("gpsimd",
                          dap(lg_in, b * T + ch * 512, [[BT, 8], [1, 512]]),
                          lgst[:], s_pp)

        def classify(kt, qc):
            d0 = 128 * kt - 512 * qc
            if d0 <= -256:
                return ("LO", None)
            if d0 >= 640:
                return ("HI", None)
            subs = []
            for qs in range(4):
                d = d0 - 128 * qs
                subs.append("LO" if d <= -256 else ("HI" if d >= 256 else "NEAR"))
            return ("MIX", subs)

        phase_ctr = [0]

        def emit_attn(kind):
            esA = contextlib.ExitStack()
            phase_ctr[0] += 1
            px = f"P{phase_ctr[0]}_"
            with esA:
                xnch = [sb(esA, px + f"at_xn{i}", (128, NCT * 512), BF16)
                        for i in range(2)]
                vps = sb(esA, px + "at_v", (128, NTT * VROW), BF16)
                qkw = [sb(esA, px + f"at_qkw{i}", (128, 2 * NCT * 128), BF16)
                       for i in range(2)]
                qts = sb(esA, px + "at_qt", (128, T), BF16)
                kts = sb(esA, px + "at_kt", (128, T), BF16)
                klo = sb(esA, px + "at_klo", (128, T), BF16) if kind == "b" else None
                khi = sb(esA, px + "at_khi", (128, T), BF16) if kind == "b" else None
                expt = sb(esA, px + "at_exp", (128, 8 * 512), BF16)
                skw = sb(esA, px + "at_skw", (128, 4 * 384), BF16) if kind == "b" else None
                ppsb = [sb(esA, px + f"at_pp{i}", (128, PW), BF16) for i in range(2)] \
                    if kind == "b" else None
                yraw = sb(esA, px + "at_yraw", (128, T), BF16)
                rcp = sb(esA, px + "at_rcp", (128, T), BF16)
                rbr = sb(esA, px + "at_rbr", (128, T), BF16)
                ytmp = sb(esA, px + "at_ytmp", (128, T), BF16)
                ytslab = sb(esA, px + "at_ytsl", (128, T), BF16)
                ymv = [sb(esA, px + f"at_ymv{i}", (128, NCT * 512), BF16)
                       for i in range(2)]
                bvb = sb(esA, px + "at_bv", (128, C), F32)
                wvv = sb(esA, px + "at_wv", (128, NCT * C), BF16)
                wpj = [sb(esA, px + f"at_wp{i}", (128, NCT * 128), BF16)
                       for i in range(2)]
                M_ps = esA.enter_context(nc.psum_tensor(px + "atp_M", [128, 512], F32))
                R_ps = esA.enter_context(nc.psum_tensor(px + "atp_R", [128, 512], F32))
                lg_ps = esA.enter_context(nc.psum_tensor(px + "atp_lg", [128, 512], F32))
                pj_ps = [esA.enter_context(
                    nc.psum_tensor(px + f"atp_pj{i}", [128, 512], F32)) for i in range(2)]
                ln_ps = (M_ps[0:1, :], R_ps[0:1, :], M_ps, R_ps)

                wq = wtr[kind + "q"]; wk = wtr[kind + "k"]
                wv = wtr[kind + "v"]
                wp = wtr[kind + ("p" if kind == "a" else "o")]
                btile = bias_a if kind == "a" else bias_b

                pj_dveg = [0, 0]
                pj_actg = [0, 0]
                yraw_guard = [0]
                rcp_w = [0]
                bvin = inp[f"L0_attn_{kind}_bv"]
                K.dma("gpsimd", bvb[:], dap(bvin, 0, [[0, 128], [1, C]]), s_bc)
                bv_ld = s_bc.n
                K.dma("sync", wvv[:].rearrange("p (n w) -> p n w", n=NCT),
                      dap(wv, 0, [[C, 128], [128 * C, NCT], [1, C]]), s_wA)
                wv_ld = s_wA.n

                for b in range(c.B_LOC):
                    emit_ln(b, ln_ps, False, lg_ps)
                    K.wait("vector", s_pe, s_pe.n)
                    vset(sap(vps, 64, [[VROW, NTT], [65, NH]]), 1.0,
                         inc=s_dve)
                    vones = s_dve.n
                    K.wait("tensor", s_wA, wv_ld)
                    for ch in range(T // 512):
                        sl = ch % 2
                        K.wait("gpsimd", s_pe, s_pe.n)
                        K.wait("gpsimd", s_xns, s_xns.n)
                        K.dma("gpsimd",
                              xnch[sl][:].rearrange("p (n w) -> p n w", n=NCT),
                              dap(xnT_d, b * C * T + ch * 512,
                                  [[T, 128], [128 * T, NCT], [1, 512]]), s_xn[sl])
                        K.wait("tensor", s_xn[sl], s_xn[sl].n)
                        for ttl in range(4):
                            tt = ch * 4 + ttl
                            for s2 in range(C // 512):
                                bk = (ttl * (C // 512) + s2) % 2
                                pp = pj_ps[bk]
                                K.wait("tensor", s_dve, pj_dveg[bk])
                                K.wait("tensor", s_act, pj_actg[bk])
                                for ct in range(NCT):
                                    mm(pp[:],
                                       xnch[sl][:, ct * 512 + ttl * 128:
                                                ct * 512 + ttl * 128 + 128],
                                       wvv[:, ct * C + s2 * 512:
                                           ct * C + s2 * 512 + 512],
                                       start=(ct == 0), stop=(ct == NCT - 1),
                                       inc=(s_pe if ct == NCT - 1 else None))
                                K.wait("vector", s_pe, s_pe.n)
                                K.wait("vector", s_dve, vones)
                                K.wait("vector", s_bc, bv_ld)
                                vtt(sap(vps, tt * VROW + s2 * 8 * 65,
                                        [[65, 512 // 64], [1, 64]]),
                                    pp[:].rearrange("p (n w) -> p n w", n=8),
                                    bvb[:, s2 * 512:(s2 + 1) * 512]
                                    .rearrange("p (n w) -> p n w", n=8),
                                    ALU.add, inc=s_dve)
                                pj_dveg[bk] = s_dve.n
                    vdone = s_dve.n
                    sc_guard = {0: 0, 1: 0}
                    for hp in range(NHP):
                        slw = hp % 2
                        K.wait("sync", s_pe, s_pe.n)
                        K.dma("sync", qkw[slw][:, 0:NCT * 128],
                              dap(wq, hp * 128 * C, [[C, 128], [1, NCT * 128]]),
                              s_wB)
                        K.dma("sync", qkw[slw][:, NCT * 128:2 * NCT * 128],
                              dap(wk, hp * 128 * C, [[C, 128], [1, NCT * 128]]),
                              s_wB)
                        qk_ld = s_wB.n
                        K.wait("tensor", s_wB, qk_ld)
                        K.wait("tensor", s_dve, vdone)
                        for qc in range(NQC):
                            sl = qc % 2
                            K.wait("gpsimd", s_pe, s_pe.n)
                            K.wait("gpsimd", s_xns, s_xns.n)
                            K.dma("gpsimd",
                                  xnch[sl][:].rearrange("p (n w) -> p n w", n=NCT),
                                  dap(xnT_d, b * C * T + qc * 512,
                                      [[T, 128], [128 * T, NCT], [1, 512]]),
                                  s_xn[sl])
                            K.wait("tensor", s_xn[sl], s_xn[sl].n)
                            for which in (0, 1):
                                pp = pj_ps[which]
                                K.wait("tensor", s_act, pj_actg[which])
                                K.wait("tensor", s_dve, pj_dveg[which])
                                for ct in range(NCT):
                                    mm(pp[:],
                                       qkw[slw][:, which * NCT * 128 + ct * 128:
                                                which * NCT * 128 + ct * 128 + 128],
                                       xnch[sl][:, ct * 512:(ct + 1) * 512],
                                       start=(ct == 0), stop=(ct == NCT - 1),
                                       inc=(s_pe if ct == NCT - 1 else None))
                                K.wait("scalar", s_pe, s_pe.n)
                                dst = qts if which == 0 else kts
                                bcol = btile[:, which * NCT + hp:
                                             which * NCT + hp + 1]
                                act(dst[:, qc * 512:(qc + 1) * 512], pp[:],
                                    AF.Identity, bias=bcol, inc=s_act)
                                if kind == "b" and which == 1:
                                    act(klo[:, qc * 512:(qc + 1) * 512], pp[:],
                                        AF.Identity, bias=bklo[:, hp:hp + 1],
                                        inc=s_act)
                                    act(khi[:, qc * 512:(qc + 1) * 512], pp[:],
                                        AF.Identity, bias=bkhi[:, hp:hp + 1],
                                        inc=s_act)
                                pj_actg[which] = s_act.n
                        qk_act = s_act.n
                        pp_done = [0, 0]
                        if kind == "b":
                            K.wait("tensor", s_act, qk_act)
                            K.wait("sync", s_sk, s_sk.n)
                            for hi2 in range(2):
                                r2 = 64 * hi2
                                ppd = pp_d[hi2]
                                for qt in range(NTT):
                                    ii = (hi2 * NTT + qt) % 2
                                    pp = pj_ps[ii]
                                    psl = ppsb[ii]
                                    K.wait("tensor", s_dve, pj_dveg[ii])
                                    K.wait("tensor", s_act, pj_actg[ii])
                                    mm(pp[:, 0:PW],
                                       qts[r2:r2 + 64, qt * 128:(qt + 1) * 128],
                                       rel_ext[r2:r2 + 64, :], start=True,
                                       stop=True, inc=s_pe)
                                    K.wait("vector", s_pe, s_pe.n)
                                    K.wait("vector", s_ppw, s_ppw.n)
                                    vcopy(psl[:, 0:PW], pp[:, 0:PW], inc=s_dve)
                                    pj_dveg[ii] = s_dve.n
                                    K.wait("sync", s_dve, s_dve.n)
                                    K.dma("sync",
                                          dap(ppd, qt * 128 * PW,
                                              [[PW, 128], [1, PW]]),
                                          psl[:, 0:PW], s_ppw)
                                pp_done[hi2] = s_ppw.n
                        for hi in range(2):
                            h = 2 * hp + hi
                            r0 = 64 * hi
                            K.wait("tensor", s_act, qk_act)
                            if kind == "b":
                                ppd = pp_d[hi]
                            for qc in range(NQC):
                                if kind == "b":
                                    K.wait("sync", s_ppw, pp_done[hi])
                                    K.wait("sync", s_qk, s_qk.n)
                                    for qs in range(4):
                                        qt = qc * 4 + qs
                                        K.dma("sync",
                                              skw[:, qs * 384:(qs + 1) * 384],
                                              dap(ppd, qt * 128 * PW + 127,
                                                  [[PW - 1, 128], [1, 384]]),
                                              s_sk)
                                    sk_ld = s_sk.n
                                av_ps = pj_ps[qc % 2]
                                sc_ps = [M_ps, R_ps]
                                exp_acts = {}
                                av_guard = s_pe.n
                                for kt in range(NTT):
                                    sp = sc_ps[kt % 2]
                                    K.wait("tensor", s_act, sc_guard[kt % 2])
                                    if kind == "a":
                                        case, subs = "PLAIN", None
                                    else:
                                        case, subs = classify(kt, qc)
                                    if case in ("PLAIN", "LO", "HI"):
                                        src = (kts if case == "PLAIN" else
                                               (klo if case == "LO" else khi))
                                        mm(sp[:],
                                           src[r0:r0 + 64, kt * 128:(kt + 1) * 128],
                                           qts[r0:r0 + 64, qc * 512:(qc + 1) * 512],
                                           start=True, stop=True, inc=s_qk)
                                    else:
                                        K.wait("tensor", s_sk, sk_ld)
                                        items = []
                                        for qs in range(4):
                                            q0 = qc * 512 + qs * 128
                                            sub = subs[qs]
                                            src = (klo if sub == "LO" else
                                                   khi if sub == "HI" else kts)
                                            items.append((
                                                sp[:, qs * 128:(qs + 1) * 128],
                                                src[r0:r0 + 64,
                                                    kt * 128:(kt + 1) * 128],
                                                qts[r0:r0 + 64, q0:q0 + 128]))
                                            if sub == "NEAR":
                                                off = 128 * kt - (q0 - 128)
                                                items.append((
                                                    sp[:, qs * 128:(qs + 1) * 128],
                                                    skw[:, qs * 384 + off:
                                                        qs * 384 + off + 128],
                                                    id128[:]))
                                        for ii, (o_, l_, r_) in enumerate(items):
                                            mm(o_, l_, r_, start=(ii == 0),
                                               stop=(ii == len(items) - 1),
                                               inc=(s_qk if ii == len(items) - 1
                                                    else None))
                                    K.wait("scalar", s_qk, s_qk.n)
                                    K.wait("scalar", s_pe, av_guard)
                                    act(expt[:, (kt % 8) * 512:(kt % 8) * 512 + 512],
                                        sp[:], AF.Exp, scale=cst[:, 0:1],
                                        inc=s_act)
                                    exp_acts[kt] = s_act.n
                                    sc_guard[kt % 2] = s_act.n
                                    if kt % 8 == 7 or kt == NTT - 1:
                                        K.wait("tensor", s_act, exp_acts[kt])
                                        K.wait("tensor", s_act,
                                               pj_actg[qc % 2])
                                        K.wait("tensor", s_dve,
                                               max(vdone, pj_dveg[qc % 2]))
                                        for k2 in range(kt - (kt % 8), kt + 1):
                                            mm(av_ps[0:65, :],
                                               sap(vps, k2 * VROW + h * 65,
                                                   [[1, 65]]),
                                               expt[:, (k2 % 8) * 512:
                                                    (k2 % 8) * 512 + 512],
                                               start=(k2 == 0),
                                               stop=(k2 == NTT - 1),
                                               inc=(s_pe if k2 == NTT - 1
                                                    else None))
                                        av_guard = s_pe.n
                                av_pe = s_pe.n
                                K.wait("vector", s_pe, av_pe)
                                K.wait("vector", s_dve, yraw_guard[0])
                                vcopy(yraw[0:64, qc * 512:(qc + 1) * 512],
                                      av_ps[0:64, :], inc=s_dve)
                                pj_dveg[qc % 2] = s_dve.n
                                K.wait("scalar", s_pe, av_pe)
                                K.wait("scalar", s_act, s_act.n)
                                K.wait("scalar", s_pp, rcp_w[0])
                                act(rcp[64:65, qc * 512:(qc + 1) * 512],
                                    av_ps[64:65, :], AF.Ln, inc=s_act)
                                K.wait("scalar", s_act, s_act.n)
                                act(rcp[64:65, qc * 512:(qc + 1) * 512],
                                    rcp[64:65, qc * 512:(qc + 1) * 512], AF.Exp,
                                    scale=cst[64:65, 5:6], inc=s_act)
                                pj_actg[qc % 2] = s_act.n
                            K.wait("gpsimd", s_act, s_act.n)
                            K.wait("gpsimd", s_rg, s_rg.n)
                            K.dma("gpsimd", dap(rc_d, 0, [[T, 1], [1, T]]),
                                  rcp[64:65, :], s_pp)
                            rcp_w[0] = s_pp.n
                            K.wait("gpsimd", s_pp, s_pp.n)
                            K.wait("gpsimd", s_dve, yraw_guard[0])
                            K.dma("gpsimd", rbr[0:64, :],
                                  dap(rc_d, 0, [[0, 64], [1, T]]), s_rg)
                            rb_ld = s_rg.n
                            K.wait("vector", s_rg, rb_ld)
                            K.wait("vector", s_dve, s_dve.n)
                            if hi == 0:
                                vtt(ytslab[0:64, :], yraw[0:64, :], rbr[0:64, :],
                                    ALU.mult, inc=s_dve)
                                yraw_guard[0] = s_dve.n
                            else:
                                vtt(ytmp[0:64, :], yraw[0:64, :], rbr[0:64, :],
                                    ALU.mult, inc=s_dve)
                                yraw_guard[0] = s_dve.n
                                K.wait("gpsimd", s_dve, s_dve.n)
                                K.dma("gpsimd",
                                      dap(yodd_d, 0, [[T, 64], [1, T]]),
                                      ytmp[0:64, :], s_pp)
                                K.wait("gpsimd", s_pp, s_pp.n)
                                K.dma("gpsimd", ytslab[64:128, :],
                                      dap(yodd_d, 0, [[T, 64], [1, T]]), s_rg)
                                ysl_ld = s_rg.n
                                K.wait("gpsimd", s_rg, ysl_ld)
                                K.wait("gpsimd", s_dve, s_dve.n)
                                K.dma("gpsimd",
                                      dap(yT_d, hp * 128 * T, [[T, 128], [1, T]]),
                                      ytslab[:], s_xns)
                    yt_done = s_xns.n
                    for cc in range(T // 512):
                        slx = cc % 2
                        K.wait("gpsimd", s_dve, s_dve.n)
                        K.wait("gpsimd", s_pe, s_pe.n)
                        K.wait("gpsimd", s_xst, s_xst.n)
                        K.dma("gpsimd",
                              xtb[slx][:].rearrange("p (n w) -> p n w", n=NCT),
                              dap(xT_d, b * C * T + cc * 512,
                                  [[T, 128], [128 * T, NCT], [1, 512]]),
                              s_xt[slx])
                        K.wait("gpsimd", s_xns, yt_done)
                        K.dma("gpsimd",
                              ymv[slx][:].rearrange("p (n w) -> p n w", n=NCT),
                              dap(yT_d, cc * 512,
                                  [[T, 128], [128 * T, NCT], [1, 512]]),
                              s_xn[slx])
                        for cpt in range(NCT):
                            slw2 = cpt % 2
                            K.wait("sync", s_pe, s_pe.n)
                            K.dma("sync", wpj[slw2][:],
                                  dap(wp, cpt * 128 * C,
                                      [[C, 128], [1, NCT * 128]]), s_wB)
                            K.wait("tensor", s_wB, s_wB.n)
                            K.wait("tensor", s_xn[slx], s_xn[slx].n)
                            K.wait("tensor", s_dve, pj_dveg[cpt % 2])
                            K.wait("tensor", s_act, pj_actg[cpt % 2])
                            pp = pj_ps[cpt % 2]
                            for ct in range(NCT):
                                mm(pp[:], wpj[slw2][:, ct * 128:(ct + 1) * 128],
                                   ymv[slx][:, ct * 512:(ct + 1) * 512],
                                   start=(ct == 0), stop=(ct == NCT - 1),
                                   inc=(s_pe if ct == NCT - 1 else None))
                            K.wait("vector", s_pe, s_pe.n)
                            K.wait("vector", s_xt[slx], s_xt[slx].n)
                            bcol = btile[:, 3 * NCT + cpt: 3 * NCT + cpt + 1]
                            vstt(xtb[slx][:, cpt * 512:(cpt + 1) * 512], pp[:],
                                 bcol, xtb[slx][:, cpt * 512:(cpt + 1) * 512],
                                 ALU.add, ALU.add, inc=s_dve)
                            pj_dveg[cpt % 2] = s_dve.n
                        K.wait("gpsimd", s_dve, s_dve.n)
                        K.dma("gpsimd", dap(xT_d, b * C * T + cc * 512,
                                            [[T, 128], [128 * T, NCT], [1, 512]]),
                              xtb[slx][:].rearrange("p (n w) -> p n w", n=NCT),
                              s_xst)
                K.barrier()

        def emit_ffn():
            esF = contextlib.ExitStack()
            phase_ctr[0] += 1
            px = f"P{phase_ctr[0]}_"
            with esF:
                xnu = [sb(esF, px + f"ff_xn{i}", (128, NCT * 512), BF16)
                       for i in range(2)]
                ght = sb(esF, px + "ff_gh", (128, NHT * 512), BF16)
                accs = sb(esF, px + "ff_acc", (128, NCT * 512), F32)
                gG = sb(esF, px + "ff_g", (128, 512), F32)
                gbu = [sb(esF, px + f"ff_gb{i}", (8, 512), BF16) for i in range(2)]
                glog = sb(esF, px + "ff_glog", (128, c.GL), F32)
                bisv = sb(esF, px + "ff_bisv", (128, c.GL), F32)
                gcb = sb(esF, px + "ff_gcb", (8, 2048), BF16)
                wslot = [sb(esF, px + f"ff_w{i}", (128, HFF * C // 512), BF16)
                         for i in range(2)]
                M_ps = esF.enter_context(nc.psum_tensor(px + "ffp_M", [128, 512], F32))
                R_ps = esF.enter_context(nc.psum_tensor(px + "ffp_R", [128, 512], F32))
                lg_ps = esF.enter_context(nc.psum_tensor(px + "ffp_lg", [128, 512], F32))
                hps = [esF.enter_context(
                    nc.psum_tensor(px + f"ffp_h{i}", [128, 512], F32)) for i in range(2)]
                ops = [esF.enter_context(
                    nc.psum_tensor(px + f"ffp_o{i}", [128, 512], F32)) for i in range(2)]
                cn_ps = lg_ps[0:128, 0:1]
                ln_ps = (M_ps[0:1, :], R_ps[0:1, :], M_ps, R_ps)

                for b in range(c.B_LOC):
                    emit_ln(b, ln_ps, True, lg_ps)
                K.wait("gpsimd", s_pp, s_pp.n)
                K.op("gpsimd", lambda E: E.collective_compute(
                    "AllGather", ALU.bypass, replica_groups=[list(range(8))],
                    ins=[dap(lg_in, 0, [[BT, 8], [1, BT]])],
                    outs=[dap(lg_out, 0, [[BT, 64], [1, BT]])]), inc=s_cc)
                K.wait("gpsimd", s_cc, 1 if s_cc.n <= 1 else s_cc.n)
                K.dma("gpsimd", glog[:],
                      dap(lg_out, 0, [[c.GL, 128], [1, c.GL]]), s_bc)
                gl_ld = s_bc.n
                K.wait("vector", s_bc, gl_ld)
                vset(bis[:, 0:1], -64.0)
                vset(bis[:, 1:2], 64.0, inc=s_dve)
                for it in range(c.NBISECT):
                    K.wait("vector", s_dve, s_dve.n)
                    vtt(bis[:, 2:3], bis[:, 0:1], bis[:, 1:2], ALU.add, inc=s_dve)
                    K.wait("vector", s_dve, s_dve.n)
                    vts(bis[:, 2:3], bis[:, 2:3], cst[:, 6:7], ALU.mult,
                        inc=s_dve)
                    K.wait("vector", s_dve, s_dve.n)
                    vts(bisv[:], glog[:], bis[:, 2:3], ALU.is_le, inc=s_dve)
                    K.wait("vector", s_dve, s_dve.n)
                    vred(bcnt[:], bisv[:], inc=s_dve)
                    cnt_dve = s_dve.n
                    K.wait("tensor", s_dve, cnt_dve)
                    mm(cn_ps, onesf_st[:], bcnt[:], start=True, stop=True,
                       inc=s_pe)
                    K.wait("vector", s_pe, s_pe.n)
                    vcopy(bis[:, 3:4], cn_ps, inc=s_dve)
                    K.wait("vector", s_dve, s_dve.n)
                    vts(bprd[:], bis[:, 3:4], cst[:, 4:5], ALU.is_ge)
                    vts(bprd2[:], bis[:, 3:4], cst[:, 4:5], ALU.is_lt, inc=s_dve)
                    K.wait("vector", s_dve, s_dve.n)
                    vpred(bis[:, 1:2], bprd[:], bis[:, 2:3])
                    vpred(bis[:, 0:1], bprd2[:], bis[:, 2:3], inc=s_dve)
                SEG = min(2048, BT, c.GL)
                for si in range(BT // SEG):
                    K.wait("gpsimd", s_pp, s_pp.n)
                    K.wait("gpsimd", s_dve, s_dve.n)
                    K.dma("gpsimd", glog[0:8, 0:SEG],
                          dap(lg_in, si * SEG, [[BT, 8], [1, SEG]]), s_bc)
                    K.wait("vector", s_bc, s_bc.n)
                    vts(bisv[0:8, 0:SEG], glog[0:8, 0:SEG], bis[0:8, 0:1],
                        ALU.is_gt, inc=s_dve)
                    K.wait("vector", s_dve, s_dve.n)
                    vtt(glog[0:8, 0:SEG], glog[0:8, 0:SEG],
                        bisv[0:8, 0:SEG], ALU.mult, inc=s_dve)
                    K.wait("vector", s_dve, s_dve.n)
                    vcopy(gcb[:, 0:SEG], glog[0:8, 0:SEG], inc=s_dve)
                    K.wait("gpsimd", s_dve, s_dve.n)
                    K.dma("gpsimd", dap(gts_d, si * SEG, [[BT, 8], [1, SEG]]),
                          glog[0:8, 0:SEG], s_pp)
                    K.dma("gpsimd", dap(gtsb_d, si * SEG, [[BT, 8], [1, SEG]]),
                          gcb[:, 0:SEG], s_pp)
                gates_dve = s_dve.n
                gts_done = s_pp.n

                slot_seq = []
                for u in range(c.NU):
                    for e in range(E):
                        for q in range(4):
                            slot_seq.append(("w1", e, q))
                        for q in range(4):
                            slot_seq.append(("w2", e, q))
                slot_free_pe = [0, 0]
                slot_sem = [s_wA, s_wB]
                slot_loaded = [0, 0]
                seq_i = [0]

                def issue_load():
                    i = seq_i[0]
                    if i >= len(slot_seq):
                        return
                    seq_i[0] = i + 1
                    slx = i % 2
                    kindw, e, q = slot_seq[i]
                    K.wait("sync", s_pe, slot_free_pe[slx])
                    if kindw == "w1":
                        K.wait("sync", s_ts[0], w1t_done[e][0])
                        K.wait("sync", s_ts[1], w1t_done[e][1])
                        K.dma("sync",
                              wslot[slx][:].rearrange("p (n w) -> p n w",
                                                      w=HFF // 4),
                              dap(w1_t2, e * HFF * C + q * (HFF // 4),
                                  [[HFF, 128], [128 * HFF, NCT], [1, HFF // 4]]),
                              slot_sem[slx])
                    else:
                        K.wait("sync", s_ts[0], w2t_done[e][0])
                        K.wait("sync", s_ts[1], w2t_done[e][1])
                        K.dma("sync",
                              wslot[slx][:].rearrange("p (n w) -> p n w",
                                                      w=C // 4),
                              dap(w2_t, e * C * HFF + q * (C // 4),
                                  [[C, 128], [128 * C, NHT], [1, C // 4]]),
                              slot_sem[slx])
                    slot_loaded[slx] = slot_sem[slx].n

                issue_load(); issue_load()

                act_guard = [0, 0]
                opg = [0, 0]
                ghg = {}
                for u in range(c.NU):
                    b, ch = u // (T // 512), u % (T // 512)
                    su = u % 2
                    K.wait("gpsimd", s_pe, s_pe.n)
                    K.wait("gpsimd", s_xns, s_xns.n)
                    K.dma("gpsimd", xnu[su][:].rearrange("p (n w) -> p n w", n=NCT),
                          dap(xnT_d, b * C * T + ch * 512,
                              [[T, 128], [128 * T, NCT], [1, 512]]), s_xn[su])
                    K.wait("gpsimd", s_pp, gts_done)
                    K.dma("gpsimd", gbu[su][:],
                          dap(gtsb_d, b * T + ch * 512, [[BT, 8], [1, 512]]),
                          s_xn[su])
                    xn_ld = s_xn[su].n
                    K.wait("gpsimd", s_dve, s_dve.n)
                    K.wait("gpsimd", s_xst, s_xst.n)
                    K.dma("gpsimd", accs[:].rearrange("p (n w) -> p n w", n=NCT),
                          dap(xT_d, b * C * T + ch * 512,
                              [[T, 128], [128 * T, NCT], [1, 512]]), s_xt[0])
                    acc_ld = s_xt[0].n
                    for e in range(E):
                        K.wait("gpsimd", s_dve, s_dve.n)
                        K.wait("gpsimd", s_pp, gts_done)
                        K.dma("gpsimd", gG[:],
                              dap(gts_d, e * BT + b * T + ch * 512,
                                  [[0, 128], [1, 512]]), s_bc)
                        g_ld = s_bc.n
                        for qtr in range(4):
                            slx = (u * E * 8 + e * 8 + qtr) % 2
                            K.wait("tensor", slot_sem[slx], slot_loaded[slx])
                            if qtr == 0 and e == 0:
                                K.wait("tensor", s_xn[su], xn_ld)
                            for htl in range(NHT // 4):
                                ht = qtr * (NHT // 4) + htl
                                hpp = hps[ht % 2]
                                K.wait("tensor", s_act, act_guard[ht % 2])
                                K.wait("tensor", s_dve, ghg.get(ht, 0))
                                for ct in range(NCT):
                                    blk = ct * (NHT // 4) + htl
                                    mm(hpp[:],
                                       wslot[slx][:, blk * 128: blk * 128 + 128],
                                       xnu[su][:, ct * 512:(ct + 1) * 512],
                                       start=(ct == 0), stop=(ct == NCT - 1),
                                       inc=(s_pe if ct == NCT - 1 else None))
                                K.wait("scalar", s_pe, s_pe.n)
                                K.wait("scalar", s_dve, ghg.get(ht, 0))
                                act(ght[:, ht * 512:(ht + 1) * 512], hpp[:],
                                    AF.Gelu_apprx_tanh,
                                    bias=b1t[:, e * NHT + ht: e * NHT + ht + 1],
                                    inc=s_act)
                                act_guard[ht % 2] = s_act.n
                                K.wait("vector", s_act, s_act.n)
                                K.wait("vector", s_bc, g_ld)
                                vtt(ght[:, ht * 512:(ht + 1) * 512],
                                    ght[:, ht * 512:(ht + 1) * 512], gG[:],
                                    ALU.mult, inc=s_dve)
                                ghg[ht] = s_dve.n
                            slot_free_pe[slx] = s_pe.n
                            issue_load()
                        gh_done = s_dve.n
                        K.wait("tensor", s_dve, gh_done)
                        for qtr in range(4):
                            slx = (u * E * 8 + e * 8 + 4 + qtr) % 2
                            K.wait("tensor", slot_sem[slx], slot_loaded[slx])
                            for dtl in range(NDT // 4):
                                dt = qtr * (NDT // 4) + dtl
                                op = ops[dt % 2]
                                K.wait("tensor", s_dve, opg[dt % 2])
                                if e == 0:
                                    K.wait("tensor", s_xn[su], xn_ld)
                                    mm(op[:], b2bf[0:8, dt * 128:(dt + 1) * 128],
                                       gbu[su][:],
                                       start=True, stop=False)
                                for ht in range(NHT):
                                    blk = ht * (NDT // 4) + dtl
                                    mm(op[:],
                                       wslot[slx][:, blk * 128: blk * 128 + 128],
                                       ght[:, ht * 512:(ht + 1) * 512],
                                       start=(ht == 0 and e != 0),
                                       stop=(ht == NHT - 1),
                                       inc=(s_pe if ht == NHT - 1 else None))
                                K.wait("vector", s_pe, s_pe.n)
                                if e == 0:
                                    K.wait("vector", s_xt[0], acc_ld)
                                vtt(accs[:, dt * 512:(dt + 1) * 512], op[:],
                                    accs[:, dt * 512:(dt + 1) * 512], ALU.add,
                                    inc=s_dve)
                                opg[dt % 2] = s_dve.n
                            slot_free_pe[slx] = s_pe.n
                            issue_load()
                    K.wait("gpsimd", s_dve, s_dve.n)
                    K.dma("gpsimd", dap(xT_d, b * C * T + ch * 512,
                                        [[T, 128], [128 * T, NCT], [1, 512]]),
                          accs[:].rearrange("p (n w) -> p n w", n=NCT), s_xst)
            K.barrier()

        w1t_done = {}
        w2t_done = {}

        def emit_w2prep():
            for e in range(E):
                for ct in range(NCT):
                    slot = ct % 2
                    K.wait("sync", s_prep, prep_cast_done)
                    K.wait("sync", s_ts[slot], s_ts[slot].n)
                    K.dma_t("sync", ttmpw[slot][:, 0:HFF],
                            dap(w1_nat, e * HFF * C + ct * 128,
                                [[C, HFF], [1, 128]]), s_tl[slot])
                    K.wait("sync", s_tl[slot], s_tl[slot].n)
                    K.dma("sync", dap(w1_t2, e * HFF * C + ct * 128 * HFF,
                                      [[HFF, 128], [1, HFF]]),
                          ttmpw[slot][:, 0:HFF], s_ts[slot])
                w1t_done[e] = (s_ts[0].n, s_ts[1].n)
                for ht in range(NHT):
                    slot = ht % 2
                    K.wait("sync", s_ts[slot], s_ts[slot].n)
                    K.dma_t("sync", ttmpw[slot][:, 0:C],
                            dap(w2_nat, e * C * HFF + ht * 128,
                                [[HFF, C], [1, 128]]), s_tl[slot])
                    K.wait("sync", s_tl[slot], s_tl[slot].n)
                    K.dma("sync", dap(w2_t, e * C * HFF + ht * 128 * C,
                                      [[C, 128], [1, C]]),
                          ttmpw[slot][:, 0:C], s_ts[slot])
                w2t_done[e] = (s_ts[0].n, s_ts[1].n)

        plan = ["a", "f", "b", "f"][:n_subl]
        w2p = [False]
        for ph in plan:
            if ph == "f" and not w2p[0]:
                emit_w2prep()
                esW.close()
                w2p[0] = True
            if ph == "f":
                emit_ffn()
            else:
                emit_attn(ph)
            if not w2p[0]:
                emit_w2prep()
                esW.close()
                w2p[0] = True
        if not w2p[0]:
            esW.close()

        esO = contextlib.ExitStack()
        with esO:
            xin = sb(esO, "o_xin", (128, C), F32)
            xout = sb(esO, "o_xout", (128, C), F32)
            to_ps = [esO.enter_context(nc.psum_tensor(f"to_ps{i}", [128, 128], F32))
                     for i in range(2)]
            for b in range(c.B_LOC):
                for tt in range(NTT):
                    K.wait("sync", s_io, s_io.n)
                    K.wait("sync", s_dve, s_dve.n)
                    K.wait("sync", s_xst, s_xst.n)
                    K.dma("sync", xin[:].rearrange("p (n w) -> p n w", n=NCT),
                          dap(xT_d, b * C * T + tt * 128,
                              [[T, 128], [128 * T, NCT], [1, 128]]), s_h1)
                    K.wait("tensor", s_h1, s_h1.n)
                    K.wait("tensor", s_dve, s_dve.n)
                    evac_at = {}
                    for ct in range(NCT):
                        pp = to_ps[ct % 2]
                        if ct >= 2:
                            K.wait("tensor", s_dve, evac_at[ct - 2])
                        tpose(pp[:], xin[:, ct * 128:(ct + 1) * 128], id128f[:],
                              inc=s_pe)
                        K.wait("vector", s_pe, s_pe.n)
                        vcopy(xout[:, ct * 128:(ct + 1) * 128], pp[:], inc=s_dve)
                        evac_at[ct] = s_dve.n
                    K.wait("sync", s_dve, s_dve.n)
                    K.dma("sync",
                          dap(out, (b * T + tt * 128) * C, [[C, 128], [1, C]]),
                          xout[:], s_io)
            if dump_xt:
                K.wait("sync", s_xst, s_xst.n)
                K.dma("sync", dap(xt_dump, 0, [[1, c.B_LOC * C * T]]),
                      dap(xT_d, 0, [[1, c.B_LOC * C * T]]), s_io)
            K.wait("sync", s_io, s_io.n)

        with nc.Block() as block:
            @block.sync
            def _(E):
                for f in K.q["sync"]:
                    f(E)

            @block.tensor
            def _(E):
                for f in K.q["tensor"]:
                    f(E)

            @block.vector
            def _(E):
                for f in K.q["vector"]:
                    f(E)

            @block.scalar
            def _(E):
                for f in K.q["scalar"]:
                    f(E)

            @block.gpsimd
            def _(E):
                for f in K.q["gpsimd"]:
                    f(E)
    return nc


def kernel(**inputs):
    cfg = Cfg(mini=False)
    nc = build_program(cfg, n_subl=4)
    x = np.ascontiguousarray(inputs["L0_attn_a_x"])
    shards = np.split(x, 8, axis=0)
    base = {k: np.ascontiguousarray(v) for k, v in inputs.items()
            if k != "L0_attn_a_x"}
    in_maps = [dict(base, L0_attn_a_x=shards[i]) for i in range(8)]
    res = run_bass_kernel_spmd(nc, in_maps, core_ids=list(range(8)))
    return np.concatenate([res.results[i]["out"] for i in range(8)], axis=0)
